# version 07
# baseline (speedup 1.0000x reference)
import numpy as np
import concourse.bass as bass
import concourse.mybir as mybir
from concourse.bass_utils import run_bass_kernel_spmd

F32 = mybir.dt.float32
BF16 = mybir.dt.bfloat16
AX = mybir.AxisListType.X
ALU = mybir.AluOpType
AF = mybir.ActivationFunctionType

S = 2048
E = 2048
I = 4096
NS = S // 128
NE = E // 128
NI = I // 128
NG = E // 128
EPS = 1e-5


class Ctr:
    def __init__(self, nc, name):
        self.guard = nc.semaphore(name)
        self.h = self.guard.__enter__()
        self.n = 0


class B:
    ENGINES = ("sync", "scalar", "vector", "tensor", "gpsimd")

    def __init__(self, nc):
        self.nc = nc
        self.streams = {e: [] for e in self.ENGINES}
        self.sems = {}

    def sem(self, name):
        if name not in self.sems:
            self.sems[name] = Ctr(self.nc, name)
        return self.sems[name]

    def waits(self, eng, toks):
        best = {}
        for t in toks:
            if t is None:
                continue
            s, v = t
            if s not in best or best[s] < v:
                best[s] = v
        for s, v in best.items():
            self.streams[eng].append(lambda e, s=s, v=v: e.wait_ge(s.h, v))

    def op(self, eng, fn, waits=(), sem=None):
        self.waits(eng, waits)
        s = self.sem(sem or ("c_" + eng))
        s.n += 1
        v = s.n
        self.streams[eng].append(lambda e, fn=fn, s=s: fn(e).then_inc(s.h, 1))
        return (s, v)

    def dma(self, eng, ch, out, in_, waits=(), **kw):
        self.waits(eng, waits)
        s = self.sem("d_" + ch)
        s.n += 16
        v = s.n
        self.streams[eng].append(
            lambda e, out=out, in_=in_, s=s, kw=kw:
            e.dma_start(out=out, in_=in_, **kw).then_inc(s.h, 16))
        return (s, v)

    def dmat(self, eng, ch, out, in_, waits=()):
        self.waits(eng, waits)
        s = self.sem("d_" + ch)
        s.n += 16
        v = s.n
        self.streams[eng].append(
            lambda e, out=out, in_=in_, s=s:
            e.dma_start_transpose(out=out, in_=in_).then_inc(s.h, 16))
        return (s, v)

    def emit(self):
        with self.nc.Block() as blk:
            for name in self.ENGINES:
                items = self.streams[name]
                if not items:
                    continue
                getattr(blk, name)(lambda e, items=items: [f(e) for f in items])


def build_program(upto=9, debug=()):
    nc = bass.Bass("TRN2", target_bir_lowering=False, debug=False)
    b = B(nc)

    x_d = nc.dram_tensor("x", [S, E], F32, kind="ExternalInput").ap()
    wq_d = nc.dram_tensor("wq", [E, E], F32, kind="ExternalInput").ap()
    wk_d = nc.dram_tensor("wk", [E, E], F32, kind="ExternalInput").ap()
    wv_d = nc.dram_tensor("wv", [E, E], F32, kind="ExternalInput").ap()
    wo_d = nc.dram_tensor("wo", [E, E], F32, kind="ExternalInput").ap()
    wup_d = nc.dram_tensor("wup", [I, E], F32, kind="ExternalInput").ap()
    wdn_d = nc.dram_tensor("wdn", [E, I], F32, kind="ExternalInput").ap()
    out_d = nc.dram_tensor("out", [S, E], F32, kind="ExternalOutput").ap()

    def scratch(name, shape, dtype=BF16):
        kind = "ExternalOutput" if name in debug else "Internal"
        return nc.dram_tensor(name, shape, dtype, kind=kind).ap()

    xn_d = scratch("xn_s", [128, NE, S])
    xq_d = scratch("xq_s", [NS, 128, NE, 128])
    q_d = scratch("q_s", [NE, 128, NS, 128])
    ot_d = scratch("ot_s", [NE, 128, S])
    sat_d = scratch("sat_s", [NS, 128, NG, 128])
    g_d = scratch("g_s", [NS, 128, NG, 128])
    tdt_d = scratch("tdt_s", [NG, 128, NI, 128])
    hq_d = scratch("hq_s", [128, NI, S])

    live = {}

    def push(name, shape, dtype, **kw):
        g = nc.sbuf_tensor(name, shape, dtype, **kw)
        h = g.__enter__()
        live[name] = g
        return h

    def pop(*names):
        for name in names:
            live.pop(name).__exit__(None, None, None)

    def ppush(name, shape):
        g = nc.psum_tensor(name, shape, F32)
        h = g.__enter__()
        live[name] = g
        return h

    st6 = push("st6", [128, 24], F32)
    mv = push("mv", [128, 2], F32)
    sdv = push("sdv", [128, 1], F32)
    rstd = push("rstd", [128, 1], F32)
    amax = push("amax", [128, 1], F32)
    qs_all = push("qs_all", [128, NS], F32)
    qr_all = push("qr_all", [128, NS], F32)
    mrow = push("mrow", [128, 1], F32)
    nmrow = push("nmrow", [128, 1], F32)
    zsum = push("zsum", [128, 1], F32)
    rz_all = push("rz_all", [128, NE], F32)
    wacc_up = push("wacc_up", [128, 1], F32)
    wacc_dn = push("wacc_dn", [128, 1], F32)
    wred = push("wred", [128, 1], F32)
    wred2 = push("wred2", [128, 1], F32)
    ones_f = push("ones_f", [1, 128], F32)
    onescol = push("onescol", [128, 1], F32)
    srow = push("srow", [1, 8], F32)
    sup1 = push("sup1", [1, 1], F32)
    sdn1 = push("sdn1", [1, 1], F32)
    bcast = push("bcast", [128, 8], F32)
    rmax = push("rmax", [128, 2], F32)
    rmx = push("rmx", [128, 1], F32)
    maxh = push("maxh", [128, 1], F32)
    rcph = push("rcph", [128, 1], F32)
    t1 = push("t1", [128, 1], F32)
    u1 = push("u1", [128, 1], F32)
    qs2 = push("qs2", [128, 1], F32)
    cvec = push("cvec", [128, 1], F32)
    cpool = push("cpool", [128, 2], F32)

    pA = ppush("pA", [128, 2048])
    pB = ppush("pB", [128, 2048])

    T = {}

    tk = b.op("gpsimd", lambda e: e.memset(cpool[:, 0:1], EPS))
    tk = b.op("gpsimd", lambda e: e.memset(cpool[:, 1:2], 256.5), waits=[tk])
    nc.const_aps.register(F32, EPS, cpool[:, 0:1])
    nc.const_aps.register(F32, 256.5, cpool[:, 1:2])
    wz_up = b.op("vector", lambda e: e.memset(wacc_up[:], 0.0))
    wz_dn = b.op("vector", lambda e: e.memset(wacc_dn[:], 0.0), waits=[wz_up])
    tk = b.op("gpsimd", lambda e: e.memset(ones_f[:], 1.0), waits=[tk])
    tk = b.op("gpsimd", lambda e: e.memset(onescol[:], 1.0), waits=[tk])
    gp_init = tk

    wup_t = wup_d.rearrange("(t p) e -> t p e", p=128)
    wdn_t = wdn_d.rearrange("(t p) i -> t p i", p=128)
    wstat = [None, None]
    ws_state = {"i": 0, "buf_tok": [None, None], "acc_up": None, "acc_dn": None}

    def stream_wstats(n):
        st_ = ws_state
        for _ in range(n):
            i = st_["i"]
            if i >= 64:
                return
            buf = wstat[i % 2]
            if i < 32:
                src = wup_t[i][:, :]
                acc, red = wacc_up, wred
            else:
                j = i - 32
                src = wdn_t[j // 2][:, (j % 2) * 2048:(j % 2 + 1) * 2048]
                acc, red = wacc_dn, wred2
            ld = b.dma("gpsimd", f"ws{i % 2}", buf[:], src,
                       waits=[st_["buf_tok"][i % 2]])
            key = "acc_up" if i < 32 else "acc_dn"
            init = st_[key] if st_[key] else (wz_up if i < 32 else wz_dn)
            tok = b.op("vector", lambda e, buf=buf, red=red: e.reduce_sum(
                out=red[:], in_=buf[:], axis=AX, apply_absolute_value=True),
                waits=[ld, init])
            st_[key] = b.op("vector", lambda e, acc=acc, red=red: e.tensor_add(
                out=acc[:], in0=acc[:], in1=red[:]), waits=[tok, init])
            st_["buf_tok"][i % 2] = st_[key]
            st_["i"] += 1

    wstat[0] = push("wst0", [128, 2048], F32, side="right")
    wstat[1] = push("wst1", [128, 2048], F32, side="right")
    xin = [push(f"xin{i}", [128, E], F32) for i in range(2)]
    xnf = push("xnf32", [128, E], F32)
    xnb = [push(f"xnb{i}", [128, E], BF16) for i in range(2)]
    xqb = push("xqb", [128, E], BF16)
    xnt = push("xnt", [128, NE, 128], BF16, is_transpose_dest=True)
    xqt = push("xqt", [128, NE, 128], BF16, is_transpose_dest=True)
    ybc = push("ybc", [128, 1024], F32)
    tb = push("tb", [128, 1024], F32)
    ybi = push("ybi", [128, 1024], mybir.dt.int32)
    ccb = push("ccb", [128, 1024], F32)
    ccb2 = push("ccb2", [128, 1024], F32)

    x_t = x_d.rearrange("(t p) e -> t p e", p=128)
    for st in range(NS):
        xb = xin[st % 2]
        w = []
        if st >= 2:
            w = [T[("xn", st - 2)], T[("bn", st - 2)]]
        T[("xload", st)] = b.dma("sync", f"x{st % 2}", xb[:], x_t[st], waits=w)
        bns = []
        for c in range(4):
            w = [T[("xload", st)]]
            if c == 0 and st >= 1:
                w.append(T[("bn", st - 1)])
            bns.append(b.op("vector", lambda e, xb=xb, c=c: e.bn_stats(
                out=st6[:, c * 6:(c + 1) * 6], in_=xb[:, c * 512:(c + 1) * 512]),
                waits=w))
        w = list(bns)
        if st >= 1:
            w += [T[("sq", st - 1)], T[("xn", st - 1)]]
        T[("bn", st)] = b.op("vector", lambda e: e.bn_aggr(
            out=mv[:], in_=st6[:]), waits=w)
        T[("sq", st)] = b.op("scalar", lambda e: e.activation(
            out=sdv[:], in_=mv[:, 1:2], func=AF.Sqrt, bias=EPS),
            waits=[T[("bn", st)]] + ([gp_init] if st == 0 else []))
        w = [T[("sq", st)]]
        if st >= 1:
            w += [T[("xn", st - 1)], T[("qs", st - 1)]]
        T[("rstd", st)] = b.op("vector", lambda e: e.reciprocal(
            out=rstd[:], in_=sdv[:]), waits=w)
        xnt_b = xnb[st % 2]
        w = [T[("rstd", st)]]
        if st >= 1:
            w += [T[("absmax", st - 1)], T[("y", st - 1, 1)],
                  T[("xnb", st - 1)]]
        T[("xn", st)] = b.op("vector", lambda e, xb=xb:
                             e.tensor_scalar(
                                 out=xnf[:], in0=xb[:], scalar1=mv[:, 0:1],
                                 scalar2=rstd[:], op0=ALU.subtract,
                                 op1=ALU.mult), waits=w)
        w = [T[("xn", st)]]
        if st >= 2:
            w.append(T[("xnT", st - 2)])
        T[("xnb", st)] = b.op("vector", lambda e, xnt_b=xnt_b: e.tensor_copy(
            out=xnt_b[:], in_=xnf[:]), waits=w)
        T[("absmax", st)] = b.op("vector", lambda e: e.reduce_max(
            out=amax[:], in_=xnf[:], axis=AX, apply_absolute_value=True),
            waits=[T[("xn", st)]])
        tok = b.op("vector", lambda e: e.tensor_scalar_max(
            out=amax[:], in0=amax[:], scalar1=EPS), waits=[T[("absmax", st)]])
        tok = b.op("vector", lambda e, st=st: e.tensor_scalar_mul(
            out=qr_all[:, st:st + 1], in0=amax[:], scalar1=1.0 / 127.0),
            waits=[tok])
        tok2 = b.op("vector", lambda e: e.reciprocal(out=rstd[:], in_=amax[:]),
                    waits=[tok, T[("xn", st)]])
        T[("qs", st)] = b.op("vector", lambda e, st=st: e.tensor_scalar_mul(
            out=qs_all[:, st:st + 1], in0=rstd[:], scalar1=127.0), waits=[tok2])
        prev = None
        for ch in range(2):
            sl = slice(ch * 1024, (ch + 1) * 1024)
            w = [T[("qs", st)]]
            if prev is not None:
                w.append(prev)
            elif st >= 1:
                w.append(T[("xq", st - 1, 1)])
            ty = b.op("scalar",
                lambda e, st=st, sl=sl: e.activation(
                    out=ybc[:], in_=xnf[:, sl], func=AF.Copy,
                    scale=qs_all[:, st:st + 1]), waits=w)
            T[("y", st, ch)] = ty
            tyi = b.op("vector", lambda e: e.tensor_copy(
                out=ybi[:], in_=ybc[:]), waits=[ty])
            td = b.op("vector", lambda e: e.tensor_sub(
                out=tb[:], in0=ybc[:], in1=ybi[:]), waits=[tyi])
            tca = b.op("vector", lambda e: e.tensor_single_scalar(
                out=ccb[:], in_=tb[:], scalar=0.5, op=ALU.is_gt), waits=[td])
            tcb = b.op("vector", lambda e: e.scalar_tensor_tensor(
                out=ybc[:], in0=tb[:], scalar=1.5, in1=ccb[:],
                op0=ALU.is_gt, op1=ALU.add), waits=[tca])
            tcc = b.op("vector", lambda e: e.tensor_scalar(
                out=ccb[:], in0=tb[:], scalar1=-0.5, scalar2=-1.0,
                op0=ALU.is_lt, op1=ALU.mult), waits=[tcb])
            tcd = b.op("vector", lambda e: e.scalar_tensor_tensor(
                out=ccb2[:], in0=tb[:], scalar=-1.5, in1=ccb[:],
                op0=ALU.is_lt, op1=ALU.subtract), waits=[tcc])
            tce = b.op("vector", lambda e: e.tensor_sub(
                out=tb[:], in0=ybc[:], in1=ccb2[:]), waits=[tcd])
            wq_ = [tce]
            if ch == 0 and st >= 1:
                wq_.append(T[("xqT", st - 1)])
            T[("xq", st, ch)] = b.op("vector",
                lambda e, sl=sl: e.tensor_add(
                    out=xqb[:, sl], in0=ybi[:], in1=tb[:]), waits=wq_)
            prev = T[("xq", st, ch)]
        w = [T[("xnb", st)]]
        if st >= 1:
            w.append(T[("xnS", st - 1)])
        T[("xnT", st)] = b.dmat("sync", "xnT", xnt[:], xnt_b[:], waits=w)
        T[("xnS", st)] = b.dma("scalar", "xnS",
                               xn_d[:, :, st * 128:(st + 1) * 128], xnt[:],
                               waits=[T[("xnT", st)]])
        w = [T[("xq", st, 1)]]
        if st >= 1:
            w.append(T[("xqS", st - 1)])
        T[("xqT", st)] = b.dmat("sync", "xqT", xqt[:], xqb[:], waits=w)
        T[("xqS", st)] = b.dma("scalar", "xqS", xq_d[st], xqt[:],
                               waits=[T[("xqT", st)]])
        stream_wstats(1)

    xn_done = T[("xnS", NS - 1)]
    ph1_done = [T[("bn", NS - 1)], T[("absmax", NS - 1)],
                T[("xq", NS - 1, 1)], T[("xnS", NS - 1)], T[("xqS", NS - 1)],
                T[("sq", NS - 1)], T[("qs", NS - 1)], T[("xn", NS - 1)],
                T[("xnb", NS - 1)], T[("y", NS - 1, 1)]]
    if upto < 2:
        b.emit()
        return nc

    def proj_tm(w_dram, phase, dst_sb, psum_last, prev_done, xn_tok=None):
        wf = push(f"wf{phase}", [128, NE, E], BF16)
        xblk = [push(f"xb{phase}{i}", [128, NE, 512], BF16) for i in range(2)]
        ebv = [push(f"eb{phase}{i}", [128, E], BF16) for i in range(2)] \
            if dst_sb is None else None
        toks = {}
        wl = []
        for cg in range(4):
            wl.append(b.dma("gpsimd", f"wf{cg}",
                            wf[:, cg * 4:(cg + 1) * 4, :],
                            w_dram[cg * 512:(cg + 1) * 512, :]
                            .rearrange("(c p) f -> p c f", p=128),
                            waits=list(prev_done)))
        for sg in range(4):
            xw = [xn_tok(sg) if xn_tok else xn_done] + \
                (list(prev_done) if sg < 2 else [])
            if sg >= 2:
                xw.append(toks[("mm", sg - 2, 3)])
            xl = b.dma("sync", f"xb{sg % 2}",
                       xblk[sg % 2][:],
                       xn_d[:, :, sg * 512:(sg + 1) * 512], waits=xw)
            for stl in range(4):
                st = sg * 4 + stl
                pe = pA if st % 2 == 0 else pB
                first_w = [xl, wl[0]]
                if st >= 2:
                    first_w.append(toks[("ev", st - 2)])
                elif psum_last[st % 2] is not None:
                    first_w.append(psum_last[st % 2])
                for ce in range(NE):
                    w2 = first_w if ce == 0 else \
                        ([wl[ce // 4]] if ce % 4 == 0 else ())
                    for fb in range(4):
                        tok = b.op("tensor",
                                   lambda e, pe=pe, ce=ce, fb=fb, stl=stl,
                                   xb=xblk[sg % 2]: e.matmul(
                                       pe[:, fb * 512:(fb + 1) * 512],
                                       xb[:, ce, stl * 128:(stl + 1) * 128],
                                       wf[:, ce, fb * 512:(fb + 1) * 512],
                                       start=(ce == 0), stop=(ce == NE - 1)),
                                   waits=w2 if fb == 0 else ())
                        w2 = ()
                toks[("mm", sg, stl)] = tok
                if dst_sb is None:
                    eb = ebv[st % 2]
                    w3 = [tok]
                    if st >= 2:
                        w3.append(toks[("stq", st - 2)])
                    ev = b.op("scalar", lambda e, pe=pe, eb=eb: e.activation(
                        out=eb[:], in_=pe[:, :], func=AF.Copy), waits=w3)
                    toks[("ev", st)] = ev
                    toks[("stq", st)] = b.dma(
                        "scalar", f"qs{st % 2}",
                        q_d.rearrange("t p c j -> p t c j")[:, :, st, :],
                        eb[:].rearrange("p (t j) -> p t j", j=128),
                        waits=[ev])
                else:
                    ev = b.op("scalar", lambda e, pe=pe, st=st: e.activation(
                        out=dst_sb[:, st, :], in_=pe[:, :], func=AF.Copy),
                        waits=[tok])
                    toks[("ev", st)] = ev
            stream_wstats(6)
        res_last = {0: toks[("ev", NS - 2)], 1: toks[("ev", NS - 1)]}
        mm_last = toks[("mm", 3, 3)]
        if dst_sb is None:
            res_done = toks[("stq", NS - 1)]
            extra = [toks[("stq", NS - 2)]]
            pop(f"eb{phase}1", f"eb{phase}0", f"xb{phase}1", f"xb{phase}0",
                f"wf{phase}")
        else:
            res_done = toks[("ev", NS - 1)]
            extra = []
            pop(f"xb{phase}1", f"xb{phase}0", f"wf{phase}")
        return res_done, res_last, mm_last, extra

    q_done, plast, q_mm_last, q_extra = proj_tm(
        wq_d, "Q", None, {0: None, 1: None}, [],
        xn_tok=lambda sg: T[("xnS", sg * 4 + 3)])
    pop("ccb2", "ccb", "ybi", "tb", "ybc", "xqt", "xnt", "xqb", "xnb1",
        "xnb0", "xnf32", "xin1", "xin0")
    K_sb = push("K_sb", [128, NS, E], BF16)
    k_done, plast, k_mm_last, _ = proj_tm(
        wk_d, "K", K_sb, plast, [q_mm_last, q_done] + q_extra + ph1_done)
    stream_wstats(64)
    ws_done = [ws_state["acc_up"], ws_state["acc_dn"]]
    pop("wst1", "wst0")

    if upto < 4:
        b.emit()
        return nc

    V_sb = push("V_sb", [128, NE, S], BF16)
    XNf = push("XNf", [128, NE, E], BF16)
    wcol = [push(f"wcol{i}", [128, NE, 128], BF16) for i in range(2)]
    ph4_prev = [k_mm_last, k_done, q_done] + q_extra + ws_done
    xnl = b.dma("sync", "xnf", XNf[:], xn_d[:],
                waits=[xn_done] + ph4_prev)
    vtoks = {}
    for ft in range(NE):
        buf = wcol[ft % 2]
        w = [gp_init] + (ph4_prev if ft < 2 else [])
        if ft >= 2:
            w.append(vtoks[("mm", ft - 2, 3)])
        ldw = b.dma("gpsimd", f"wc{ft % 2}", buf[:],
                    wv_d[:, ft * 128:(ft + 1) * 128]
                    .rearrange("(c p) f -> p c f", p=128), waits=w)
        pe = pA if ft % 2 == 0 else pB
        for sb_ in range(4):
            w2 = [ldw, xnl]
            if ft >= 2:
                w2.append(vtoks[("ev", ft - 2)])
            elif plast[ft % 2] is not None:
                w2.append(plast[ft % 2])
            for ce in range(NE):
                tok = b.op("tensor", lambda e, pe=pe, ce=ce, sb_=sb_, buf=buf:
                           e.matmul(pe[:, sb_ * 512:(sb_ + 1) * 512],
                                    buf[:, ce, :],
                                    XNf[:, ce, sb_ * 512:(sb_ + 1) * 512],
                                    start=(ce == 0), stop=(ce == NE - 1)),
                           waits=w2 if ce == 0 else ())
            vtoks[("mm", ft, sb_)] = tok
        ev = b.op("scalar", lambda e, pe=pe, ft=ft: e.activation(
            out=V_sb[:, ft, :], in_=pe[:, :], func=AF.Copy),
            waits=[vtoks[("mm", ft, 3)]])
        vtoks[("ev", ft)] = ev
    v_done = vtoks[("ev", NE - 1)]
    v_plast = {0: vtoks[("ev", NE - 2)], 1: vtoks[("ev", NE - 1)]}
    v_mm_last = vtoks[("mm", NE - 1, 3)]
    pop("wcol1", "wcol0", "XNf")

    if upto < 5:
        b.emit()
        return nc

    qcol = [push(f"qcol{i}", [128, NS, 128], BF16) for i in range(2)]
    ztile = [push(f"zt{i}", [128, E], BF16) for i in range(2)]
    ztp = [push(f"ztp{i}", [128, NE, 128], BF16, is_transpose_dest=True)
           for i in range(2)]
    otile = [push(f"oti{i}", [128, E], BF16) for i in range(2)]
    mrow2 = push("mrow2", [128, 1], F32)
    zsa = push("zsa", [128, 1], F32)
    zsb = push("zsb", [128, 1], F32)
    ASET = [pA[:, 0:1024], pA[:, 1024:2048], pB[:, 0:1024]]
    ph5 = {}

    def emit_O(et):
        zp = ztp[et % 2]
        ot = otile[et % 2]
        for sb_ in range(4):
            pe = pB[:, 1024 + (sb_ % 2) * 512:1536 + (sb_ % 2) * 512]
            w2 = [ph5[("zT", et)]]
            if et == 0 and sb_ < 2:
                w2.append(v_plast[1])
            if sb_ >= 2:
                w2.append(ph5[("oev", et, sb_ - 2)])
            elif et >= 1:
                w2.append(ph5[("oev", et - 1, sb_ + 2)])
            for cf in range(NE):
                tok = b.op("tensor", lambda e, pe=pe, cf=cf, sb_=sb_, zp=zp:
                           e.matmul(pe, zp[:, cf, :],
                                    V_sb[:, cf, sb_ * 512:(sb_ + 1) * 512],
                                    start=(cf == 0), stop=(cf == NE - 1)),
                           waits=w2 if cf == 0 else ())
            ph5[("Omm", et, sb_)] = tok
            w3 = [tok, ph5[("rz", et)]]
            if et >= 2:
                w3.append(ph5[("ost", et - 2)])
            ph5[("oev", et, sb_)] = b.op(
                "scalar", lambda e, pe=pe, ot=ot, sb_=sb_, et=et: e.activation(
                    out=ot[:, sb_ * 512:(sb_ + 1) * 512], in_=pe, func=AF.Copy,
                    scale=rz_all[:, et:et + 1]), waits=w3)
        ph5[("ost", et)] = b.dma("scalar", f"os{et % 2}", ot_d[et],
                                 otile[et % 2][:],
                                 waits=[ph5[("oev", et, s_)] for s_ in range(4)])

    for et in range(NE):
        qc = qcol[et % 2]
        w = [q_done]
        if et < 2:
            w.append(v_mm_last)
        if et >= 2:
            w.append(ph5[("Amm", et - 2, 1)])
        ph5[("qld", et)] = b.dma("sync", f"qc{et % 2}", qc[:], q_d[et],
                                 waits=w)
        for h in range(2):
            k = 2 * et + h
            pe = ASET[k % 3]
            w = [ph5[("qld", et)]]
            if et == 0 and h == 0:
                w += [k_done, v_done, v_plast[0]]
            if k == 2:
                w.append(v_plast[1]) if False else None
                w += [k_done, v_done, v_plast[1]]
            if k >= 3:
                kp = k - 3
                w.append(ph5[("exph", kp // 2, kp % 2)])
            first = True
            for cs in range(NS):
                for fb in (0, 1):
                    f0 = (2 * h + fb) * 512
                    tok = b.op("tensor",
                               lambda e, pe=pe, cs=cs, f0=f0, qc=qc: e.matmul(
                                   pe[:, (f0 % 1024):(f0 % 1024) + 512],
                                   qc[:, cs, :],
                                   K_sb[:, cs, f0:f0 + 512],
                                   start=(cs == 0), stop=(cs == NS - 1)),
                               waits=w if first else ())
                    first = False
            ph5[("Amm", et, h)] = tok
            mxb = mrow if h == 0 else mrow2
            w = [tok]
            if et >= 1:
                w.append(ph5[("nm", et - 1)])
            ph5[("mxh", et, h)] = b.op("vector",
                lambda e, pe=pe, mxb=mxb: e.reduce_max(
                    out=mxb[:], in_=pe, axis=AX), waits=w)
        mxc = b.op("vector", lambda e: e.tensor_max(
            out=mrow[:], in0=mrow[:], in1=mrow2[:]),
            waits=[ph5[("mxh", et, 0)], ph5[("mxh", et, 1)]])
        w = [mxc]
        if et >= 1:
            w.append(ph5[("exph", et - 1, 1)])
        ph5[("nm", et)] = b.op("scalar", lambda e: e.activation(
            out=nmrow[:], in_=mrow[:], func=AF.Copy, scale=-0.125), waits=w)
        zt = ztile[et % 2]
        for h in range(2):
            pe = ASET[(2 * et + h) % 3]
            zs = zsa if h == 0 else zsb
            w = [ph5[("nm", et)]]
            if et >= 2 and h == 0:
                w.append(ph5[("zT", et - 2)])
            if et >= 1:
                w.append(ph5[("rz", et - 1)])
            ph5[("exph", et, h)] = b.op("scalar",
                lambda e, pe=pe, zt=zt, h=h, zs=zs: e.activation(
                    out=zt[:, h * 1024:(h + 1) * 1024], in_=pe, func=AF.Exp,
                    bias=nmrow[:], scale=0.125, accum_out=zs[:]), waits=w)
        zadd = b.op("vector", lambda e: e.tensor_add(
            out=zsa[:], in0=zsa[:], in1=zsb[:]),
            waits=[ph5[("exph", et, 0)], ph5[("exph", et, 1)]])
        ph5[("rz", et)] = b.op("vector", lambda e, et=et: e.reciprocal(
            out=rz_all[:, et:et + 1], in_=zsa[:]), waits=[zadd])
        w = [ph5[("exph", et, 0)], ph5[("exph", et, 1)]]
        if et >= 2:
            w.append(ph5[("Omm", et - 2, 3)])
        ph5[("zT", et)] = b.dmat("sync", f"zt{et % 2}", ztp[et % 2][:], zt[:],
                                 waits=w)
        if et >= 1:
            emit_O(et - 1)
    emit_O(NE - 1)
    ph5_done = [ph5[("Omm", NE - 1, 3)], ph5[("oev", NE - 1, 3)],
                ph5[("oev", NE - 1, 2)], ph5[("exph", NE - 1, 0)],
                ph5[("exph", NE - 1, 1)], ph5[("exph", NE - 2, 1)],
                ph5[("rz", NE - 1)], ph5[("Amm", NE - 1, 1)],
                ph5[("zT", NE - 1)], ph5[("zT", NE - 2)],
                ph5[("ost", NE - 1)], ph5[("ost", NE - 2)],
                ph5[("nm", NE - 1)],
                ph5[("oev", NE - 1, 0)], ph5[("oev", NE - 1, 1)]]
    pop("zsb", "zsa", "mrow2", "oti1", "oti0", "ztp1", "ztp0", "zt1",
        "zt0", "qcol1", "qcol0")
    pop("V_sb", "K_sb")

    if upto < 6:
        b.emit()
        return nc

    OTh = push("OTh", [128, NE, 1024], BF16)
    stile = [push(f"sti{i}", [128, 1024], BF16) for i in range(2)]
    wcol6 = [push(f"wc6{i}", [128, NE, 128], BF16) for i in range(2)]

    allw = [ws_state["acc_up"], ws_state["acc_dn"]]
    ph5_all = ph5_done + [ph5[("oev", NE - 1, s_)] for s_ in range(4)]
    mm1 = b.op("tensor", lambda e: e.matmul(
        pB[0:1, 1536:1537], onescol[:], wacc_up[:], start=True, stop=True),
        waits=allw + [gp_init] + ph5_all)
    mm2 = b.op("tensor", lambda e: e.matmul(
        pB[0:1, 1544:1545], onescol[:], wacc_dn[:], start=True, stop=True),
        waits=[])
    t_a = b.op("vector", lambda e: e.tensor_scalar(
        out=sup1[:], in0=pB[0:1, 1536:1537], scalar1=1.0 / (I * E),
        scalar2=EPS, op0=ALU.mult, op1=ALU.max), waits=[mm1])
    t_b = b.op("vector", lambda e: e.tensor_scalar(
        out=sdn1[:], in0=pB[0:1, 1544:1545], scalar1=1.0 / (I * E),
        scalar2=EPS, op0=ALU.mult, op1=ALU.max), waits=[mm2])
    packs = [
        b.op("vector", lambda e: e.tensor_scalar_mul(
            out=srow[:, 0:1], in0=sup1[:], scalar1=0.5), waits=[t_a]),
        b.op("vector", lambda e: e.tensor_scalar_mul(
            out=srow[:, 1:2], in0=sup1[:], scalar1=-0.5), waits=[t_a]),
        b.op("vector", lambda e: e.tensor_scalar_mul(
            out=srow[:, 2:3], in0=sdn1[:], scalar1=0.5), waits=[t_b]),
        b.op("vector", lambda e: e.tensor_scalar_mul(
            out=srow[:, 3:4], in0=sdn1[:], scalar1=-0.5), waits=[t_b]),
        b.op("vector", lambda e: e.tensor_scalar_mul(
            out=srow[:, 4:5], in0=sup1[:], scalar1=1.0), waits=[t_a]),
        b.op("vector", lambda e: e.tensor_scalar_mul(
            out=srow[:, 5:6], in0=sdn1[:], scalar1=1.0), waits=[t_b]),
        b.op("vector", lambda e: e.memset(srow[:, 6:8], 0.0)),
    ]
    mm3 = b.op("tensor", lambda e: e.matmul(
        pB[:, 1552:1560], ones_f[:], srow[:], start=True, stop=True),
        waits=packs)
    bctok = b.op("vector", lambda e: e.tensor_copy(
        out=bcast[:], in_=pB[:, 1552:1560]), waits=[mm3])

    TUA = push("TUA", [128, NE, E], BF16, is_transpose_dest=True, side="right")
    TUB = push("TUB", [128, NE, E], BF16, is_transpose_dest=True, side="right")
    wstat[0] = push("wst0b", [128, 2048], F32, side="right")
    wstat[1] = push("wst1b", [128, 2048], F32, side="right")
    mskc = push("mskc", [128, 2048], BF16)
    ttile = [push(f"tt{i}", [128, 2048], BF16) for i in range(2)]
    tstg = push("tstg", [128, 16, 128], BF16, is_transpose_dest=True)
    ph7 = {}

    def ph7_unit(u):
        if u < NI:
            ic = u
            buf = wstat[ic % 2]
            w = [bctok]
            if ic >= 2:
                w.append(ph7[("tern", ic - 2)])
            ld = b.dma("gpsimd", f"tw{ic % 2}", buf[:], wup_t[ic][:, :],
                       waits=w)
            w = [ld]
            if ic >= 1:
                w.append(ph7[("tern", ic - 1)])
            msk = b.op("vector", lambda e, buf=buf: e.tensor_single_scalar(
                out=mskc[:], in_=buf[:], scalar=bcast[:, 1:2], op=ALU.is_le),
                waits=w)
            tt = ttile[ic % 2]
            w2 = [msk]
            if ic >= 2:
                w2.append(ph7[("tT", ic - 2)])
            ph7[("tern", ic)] = b.op("vector", lambda e, buf=buf, tt=tt:
                e.scalar_tensor_tensor(
                    out=tt[:], in0=buf[:], scalar=bcast[:, 0:1],
                    in1=mskc[:], op0=ALU.is_ge, op1=ALU.subtract), waits=w2)
            dst = TUA if ic < 16 else TUB
            j = ic % 16
            ph7[("tT", ic)] = b.dmat("sync", f"tT{ic % 2}",
                                     dst[:, :, j * 128:(j + 1) * 128],
                                     tt[:], waits=[ph7[("tern", ic)]])
        else:
            k = u - NI
            gc, h = k // 2, k % 2
            buf = wstat[k % 2]
            w = [ph7[("tern", NI - 2 + k)] if k < 2 else ph7[("dtern", k - 2)]]
            ld = b.dma("gpsimd", f"tw{k % 2}", buf[:],
                       wdn_t[gc][:, h * 2048:(h + 1) * 2048], waits=w)
            w = [ld, ph7[("tern", NI - 1)] if k == 0 else ph7[("dtern", k - 1)]]
            msk = b.op("vector", lambda e, buf=buf: e.tensor_single_scalar(
                out=mskc[:], in_=buf[:], scalar=bcast[:, 3:4], op=ALU.is_le),
                waits=w)
            tt = ttile[k % 2]
            w2 = [msk]
            if k >= 2:
                w2.append(ph7[("dT", k - 2)])
            else:
                w2.append(ph7[("tT", NI - 2 + k)])
            ph7[("dtern", k)] = b.op("vector", lambda e, buf=buf, tt=tt:
                e.scalar_tensor_tensor(
                    out=tt[:], in0=buf[:], scalar=bcast[:, 2:3],
                    in1=mskc[:], op0=ALU.is_ge, op1=ALU.subtract), waits=w2)
            w3 = [ph7[("dtern", k)]]
            if k >= 1:
                w3.append(ph7[("dst", k - 1)])
            tT = b.dmat("sync", "dT", tstg[:], tt[:], waits=w3)
            ph7[("dT", k)] = tT
            ph7[("dst", k)] = b.dma(
                "gpsimd", "dst", tdt_d[gc][:, h * 16:(h + 1) * 16, :],
                tstg[:], waits=[tT])

    ph6 = {}
    units = iter(range(NI + NG * 2))
    for hf in range(2):
        olds = []
        for ce in range(NE):
            w = [ph5[("ost", ce)]] + (list(ph5_done) if hf == 0 else [])
            if hf == 1:
                w.append(ph6[("mm", 0, NG - 1, 1)])
            olds.append(b.dma("sync", "ol", OTh[:, ce, :],
                              ot_d[ce][:, hf * 1024:(hf + 1) * 1024], waits=w))
        for gt in range(NG):
            buf6 = wcol6[gt % 2]
            w = list(olds)
            if gt >= 2:
                w.append(ph6[("mm", hf, gt - 2, 1)])
            elif hf == 1:
                w.append(ph6[("mm", 0, NG - 2 + gt, 1)])
            ldw = b.dma("gpsimd", f"w6{gt % 2}", buf6[:],
                        wo_d[:, gt * 128:(gt + 1) * 128]
                        .rearrange("(c p) f -> p c f", p=128), waits=w)
            for sb_ in range(2):
                pe = pA if gt % 2 == 0 else pB
                w2 = list(olds) + [ldw]
                if gt >= 2:
                    w2.append(ph6[("sev", hf, gt - 2)])
                elif hf == 0 and gt < 2:
                    w2 += [ph5[("oev", NE - 1, s_)] for s_ in range(4)]
                    w2 += [ph5[("exph", NE - 1, 0)], ph5[("exph", NE - 1, 1)],
                           ph5[("exph", NE - 2, 1)]]
                elif hf == 1:
                    w2.append(ph6[("sev", 0, NG - 2 + gt)])
                for ce in range(NE):
                    tok = b.op("tensor",
                               lambda e, pe=pe, ce=ce, sb_=sb_, buf6=buf6:
                               e.matmul(pe[:, sb_ * 512:(sb_ + 1) * 512],
                                        buf6[:, ce, :],
                                        OTh[:, ce, sb_ * 512:(sb_ + 1) * 512],
                                        start=(ce == 0), stop=(ce == NE - 1)),
                               waits=w2 if ce == 0 else ())
                ph6[("mm", hf, gt, sb_)] = tok
            stl = stile[gt % 2]
            w3 = [ph6[("mm", hf, gt, 1)], ph6[("mm", hf, gt, 0)]]
            if gt >= 2:
                w3.append(ph6[("sst", hf, gt - 2)])
            elif hf == 1:
                w3.append(ph6[("sst", 0, NG - 2 + gt)])
            ph6[("sev", hf, gt)] = b.op(
                "scalar", lambda e, pe=pe, stl=stl: e.activation(
                    out=stl[:], in_=pe[:, 0:1024], func=AF.Copy), waits=w3)
            ph6[("sst", hf, gt)] = b.dma(
                "scalar", f"ss{gt % 2}",
                sat_d[hf * 8:(hf + 1) * 8]
                .rearrange("t p c j -> p t c j")[:, :, gt, :],
                stl[:].rearrange("p (t j) -> p t j", j=128),
                waits=[ph6[("sev", hf, gt)]])
            for _ in range(2):
                u = next(units, None)
                if u is not None:
                    ph7_unit(u)
    for u in units:
        ph7_unit(u)
    satall = [ph6[("sst", 1, NG - 1)], ph6[("sst", 1, NG - 2)]]
    ph6_last_mm = ph6[("mm", 1, NG - 1, 1)]
    ph6_lastev = {0: ph6[("sev", 1, NG - 2)], 1: ph6[("sev", 1, NG - 1)]}
    tup_done = [ph7[("tT", NI - 1)], ph7[("tT", NI - 2)]]
    tdn_done = ph7[("dst", NG * 2 - 1)]
    ph7_done = [tdn_done, ph7[("dT", NG * 2 - 1)],
                ph7[("dtern", NG * 2 - 1)], ph7[("dtern", NG * 2 - 2)],
                ph7[("dst", NG * 2 - 2)]] + tup_done
    pop("tstg", "tt1", "tt0", "mskc")
    pop("wst1b", "wst0b")
    pop("wc61", "wc60", "sti1", "sti0", "OTh")

    if upto < 8:
        b.emit()
        return nc

    xqc = [push(f"xqc{i}", [128, NE, 128], BF16) for i in range(2)]
    hqb = push("hqb", [128, I], BF16)
    hqs = push("hqs", [128, NI, 128], BF16, is_transpose_dest=True)
    rbufs = [push(f"rbuf{i}", [128, I], F32) for i in range(2)]
    tbuf = push("tbuf", [128, 1024], F32)
    dbuf = push("dbuf", [128, 1024], F32)
    hibuf = push("hibuf", [128, 1024], mybir.dt.int32)
    c2buf = push("c2buf", [128, 1024], F32)
    sgnb = push("sgnb", [128, 1024], F32)
    ph8 = {}
    for st in range(NS):
        rbuf = rbufs[st % 2]
        xc = xqc[st % 2]
        w = [T[("xqS", st)]]
        if st < 2:
            w += ph7_done + [ph6_last_mm] + satall
        if st >= 2:
            w.append(ph8[("hmm", st - 2, 1)])
        ph8[("xqld", st)] = b.dma("sync", f"xq{st % 2}", xc[:], xq_d[st],
                                  waits=w)
        for ih in range(2):
            pe = pA if ih == 0 else pB
            TU = TUA if ih == 0 else TUB
            w2 = [ph8[("xqld", st)]]
            if st >= 1:
                w2.append(ph8[("relu", st - 1, ih)])
            else:
                w2.append(bctok)
                w2.append(ph6_lastev[ih])
            first = True
            for ib in range(4):
                for ce in range(NE):
                    tok = b.op("tensor",
                               lambda e, pe=pe, ib=ib, ce=ce, xc=xc, TU=TU:
                               e.matmul(pe[:, ib * 512:(ib + 1) * 512],
                                        xc[:, ce, :],
                                        TU[:, ce, ib * 512:(ib + 1) * 512],
                                        start=(ce == 0), stop=(ce == NE - 1)),
                               waits=w2 if first else ())
                    first = False
            ph8[("hmm", st, ih)] = tok
            w3 = [tok]
            if st >= 2:
                w3.append(ph8[("lastt", st - 2)])
                w3.append(ph8[("r01", st - 2)])
            ph8[("relu", st, ih)] = b.op(
                "scalar", lambda e, pe=pe, ih=ih, rbuf=rbuf: e.activation(
                    out=rbuf[:, ih * 2048:(ih + 1) * 2048], in_=pe[:, :],
                    func=AF.Relu), waits=w3)
        r0 = b.op("vector", lambda e, rbuf=rbuf: e.reduce_max(
            out=rmax[:, 0:1], in_=rbuf[:, 0:2048], axis=AX),
            waits=[ph8[("relu", st, 0)]] +
                  ([ph8[("t1", st - 1)]] if st >= 1 else []))
        r1 = b.op("vector", lambda e, rbuf=rbuf: e.reduce_max(
            out=rmax[:, 1:2], in_=rbuf[:, 2048:4096], axis=AX),
            waits=[ph8[("relu", st, 1)]])
        rm = b.op("vector", lambda e: e.tensor_max(
            out=rmx[:], in0=rmax[:, 0:1], in1=rmax[:, 1:2]), waits=[r0, r1])
        ph8[("r01", st)] = rm
        w = [rm]
        if st >= 1:
            w.append(ph8[("qs2", st - 1)])
        ph8[("t1", st)] = b.op("vector", lambda e, st=st: e.tensor_mul(
            out=t1[:], in0=bcast[:, 4:5], in1=qr_all[:, st:st + 1]), waits=w)
        tok = b.op("vector", lambda e: e.tensor_mul(
            out=maxh[:], in0=t1[:], in1=rmx[:]),
            waits=[ph8[("t1", st)]] +
                  ([ph8[("cv", st - 1)]] if st >= 1 else []))
        tok = b.op("vector", lambda e: e.tensor_mul(
            out=maxh[:], in0=maxh[:], in1=maxh[:]), waits=[tok])
        tok = b.op("vector", lambda e: e.tensor_scalar_max(
            out=maxh[:], in0=maxh[:], scalar1=EPS), waits=[tok])
        tok_r = b.op("vector", lambda e: e.reciprocal(
            out=rcph[:], in_=maxh[:]), waits=[tok])
        tok = b.op("vector", lambda e: e.tensor_mul(
            out=u1[:], in0=t1[:], in1=t1[:]), waits=[tok_r])
        ph8[("qs2", st)] = b.op("vector", lambda e: e.tensor_scalar(
            out=qs2[:], in0=u1[:], scalar1=rcph[:], scalar2=127.0,
            op0=ALU.mult, op1=ALU.mult), waits=[tok])
        ph8[("cv", st)] = b.op("vector", lambda e: e.tensor_scalar(
            out=cvec[:], in0=maxh[:], scalar1=bcast[:, 5:6],
            scalar2=1.0 / 127.0, op0=ALU.mult, op1=ALU.mult), waits=[tok_r])
        hq_parts = []
        for ch in range(4):
            sl = slice(ch * 1024, (ch + 1) * 1024)
            w4 = [ph8[("qs2", st)]] if ch == 0 else [hq_parts[-1]]
            if ch == 0 and st >= 1:
                w4.append(ph8[("hqT", st - 1)])
            t_t = b.op("vector", lambda e, sl=sl, rbuf=rbuf:
                       e.scalar_tensor_tensor(
                           out=tbuf[:], in0=rbuf[:, sl], scalar=qs2[:],
                           in1=rbuf[:, sl], op0=ALU.mult, op1=ALU.mult),
                       waits=w4)
            t_yi = b.op("vector", lambda e: e.tensor_copy(
                out=hibuf[:], in_=tbuf[:]), waits=[t_t])
            t_d = b.op("vector", lambda e: e.tensor_sub(
                out=dbuf[:], in0=tbuf[:], in1=hibuf[:]), waits=[t_yi])
            t_a = b.op("scalar", lambda e: e.activation(
                out=tbuf[:], in_=dbuf[:], func=AF.Abs), waits=[t_d])
            t_s = b.op("scalar", lambda e: e.activation(
                out=sgnb[:], in_=dbuf[:], func=AF.Sign), waits=[t_a])
            t_w1 = b.op("vector", lambda e: e.tensor_single_scalar(
                out=c2buf[:], in_=tbuf[:], scalar=0.5, op=ALU.is_gt),
                waits=[t_a])
            t_w2 = b.op("vector", lambda e: e.scalar_tensor_tensor(
                out=dbuf[:], in0=tbuf[:], scalar=1.5, in1=c2buf[:],
                op0=ALU.is_gt, op1=ALU.add), waits=[t_w1, t_s])
            t_c = b.op("vector", lambda e: e.tensor_mul(
                out=tbuf[:], in0=sgnb[:], in1=dbuf[:]), waits=[t_w2, t_s])
            t_cp = b.op("vector", lambda e: e.tensor_single_scalar(
                out=c2buf[:], in_=tbuf[:], scalar=cvec[:], op=ALU.mult),
                waits=[t_c, ph8[("cv", st)]])
            t_hq = b.op("vector", lambda e, sl=sl: e.scalar_tensor_tensor(
                out=hqb[:, sl], in0=hibuf[:], scalar=cvec[:], in1=c2buf[:],
                op0=ALU.mult, op1=ALU.add), waits=[t_cp])
            hq_parts.append(t_hq)
        ph8[("lastt", st)] = hq_parts[-1]
        w5 = list(hq_parts)
        if st >= 1:
            w5.append(ph8[("hqSt", st - 1)])
        tok = b.dmat("sync", "hqT", hqs[:], hqb[:], waits=w5)
        ph8[("hqT", st)] = tok
        ph8[("hqSt", st)] = b.dma(
            "scalar", "hqS", hq_d[:, :, st * 128:(st + 1) * 128], hqs[:],
            waits=[tok])
    f1_done = [ph8[("hmm", NS - 1, 1)], ph8[("hmm", NS - 1, 0)],
               ph8[("hqSt", NS - 1)], ph8[("hqT", NS - 1)],
               ph8[("lastt", NS - 1)], ph8[("relu", NS - 1, 0)],
               ph8[("relu", NS - 1, 1)], ph8[("qs2", NS - 1)],
               ph8[("cv", NS - 1)], ph8[("r01", NS - 1)],
               ph8[("t1", NS - 1)]]
    pop("sgnb", "c2buf", "hibuf", "dbuf", "tbuf", "rbuf1", "rbuf0",
        "hqs", "hqb", "xqc1", "xqc0")
    pop("TUB", "TUA")

    HQf = push("HQf", [128, NI, S], BF16)
    tdts = [push(f"tdts{i}", [128, NI, 128], BF16) for i in range(2)]
    gtile = [push(f"gt{i}", [128, S], BF16) for i in range(2)]
    hql = b.dma("sync", "hql", HQf[:], hq_d[:], waits=f1_done)
    for gt in range(NG):
        tb_ = tdts[gt % 2]
        w = [hql]
        if gt < 2:
            w.append(tdn_done)
        if gt >= 2:
            w.append(ph8[("fmm", gt - 2)])
        ld = b.dma("sync", f"td{gt % 2}", tb_[:], tdt_d[gt], waits=w)
        pe = pA if gt % 2 == 0 else pB
        w2 = [ld]
        if gt >= 2:
            w2.append(ph8[("gev", gt - 2)])
        else:
            w2.append(ph8[("relu", NS - 1, gt)])
        first = True
        for sb_ in range(4):
            for ci in range(NI):
                tok = b.op("tensor", lambda e, pe=pe, ci=ci, sb_=sb_, tb_=tb_:
                           e.matmul(pe[:, sb_ * 512:(sb_ + 1) * 512],
                                    tb_[:, ci, :],
                                    HQf[:, ci, sb_ * 512:(sb_ + 1) * 512],
                                    start=(ci == 0), stop=(ci == NI - 1)),
                           waits=w2 if first else ())
                first = False
        ph8[("fmm", gt)] = tok
        gq = gtile[gt % 2]
        w3 = [tok]
        if gt >= 2:
            w3.append(ph8[("gst", gt - 2)])
        ph8[("gev", gt)] = b.op("scalar", lambda e, pe=pe, gq=gq: e.activation(
            out=gq[:], in_=pe[:, :], func=AF.Silu), waits=w3)
        ph8[("gst", gt)] = b.dma(
            "scalar", f"gs{gt % 2}",
            g_d.rearrange("t p c j -> p t c j")[:, :, gt, :],
            gq[:].rearrange("p (t j) -> p t j", j=128),
            waits=[ph8[("gev", gt)]])
    f2_done = [ph8[("gst", NG - 1)], ph8[("gst", NG - 2)],
               ph8[("fmm", NG - 1)], ph8[("gev", NG - 1)],
               ph8[("gev", NG - 2)]]
    pop("gt1", "gt0", "tdts1", "tdts0", "HQf")

    if upto < 9:
        b.emit()
        return nc

    fstg = [push(f"fstg{i}", [128, NE, 128], BF16, is_transpose_dest=True)
            for i in range(2)]
    satm = [push(f"satm{i}", [128, E], BF16) for i in range(2)]
    gtm = [push(f"gtm{i}", [128, E], BF16) for i in range(2)]
    sgb = push("sgb", [128, E], BF16)
    xinf = push("xinf", [128, E], F32)
    outt = [push(f"outt{i}", [128, E], F32) for i in range(2)]
    ph9 = {}
    for st in range(NS):
        fs = fstg[st % 2]
        sm = satm[st % 2]
        gm = gtm[st % 2]
        w = list(satall) + (list(f2_done) if st < 2 else [])
        if st >= 2:
            w.append(ph9[("sg", st - 2)])
        sld = b.dma("sync", f"f9{st % 2}", fs[:], sat_d[st], waits=w)
        sT = b.dmat("sync", f"f9{st % 2}",
                    sm[:].rearrange("p (c j) -> p c j", j=128),
                    fs[:].rearrange("p c j -> p (c j)"), waits=[sld])
        gld = b.dma("sync", f"f9{st % 2}", fs[:], g_d[st],
                    waits=[ph8[("gst", NG - 1)], ph8[("gst", NG - 2)], sT])
        gT = b.dmat("sync", f"f9{st % 2}",
                    gm[:].rearrange("p (c j) -> p c j", j=128),
                    fs[:].rearrange("p c j -> p (c j)"), waits=[gld])
        w2 = (list(f2_done) if st == 0 else []) + \
            ([ph9[("add", st - 1)]] if st >= 1 else [])
        xl = b.dma("scalar", "xr", xinf[:], x_t[st], waits=w2)
        w3 = [sT, gT]
        if st >= 1:
            w3.append(ph9[("add", st - 1)])
        ph9[("sg", st)] = b.op("vector", lambda e, sm=sm, gm=gm: e.tensor_mul(
            out=sgb[:], in0=sm[:], in1=gm[:]), waits=w3)
        ob = outt[st % 2]
        w4 = [ph9[("sg", st)], xl]
        if st >= 2:
            w4.append(ph9[("ost", st - 2)])
        ph9[("add", st)] = b.op("vector", lambda e, ob=ob: e.tensor_add(
            out=ob[:], in0=xinf[:], in1=sgb[:]), waits=w4)
        ph9[("ost", st)] = b.dma(
            "sync", f"oo{st % 2}",
            out_d.rearrange("(t p) e -> t p e", p=128)[st], ob[:],
            waits=[ph9[("add", st)]])

    b.waits("sync", [ph9[("ost", NS - 1)], ph9[("ost", NS - 2)]])
    b.emit()
    return nc


def kernel(L0_attn_x, L0_attn_Wq, L0_attn_Wk, L0_attn_Wv, L0_attn_Wo,
           L0_ffn_w_up, L0_ffn_w_down):
    xs = np.ascontiguousarray(np.asarray(L0_attn_x, dtype=np.float32))
    ws = {
        "wq": np.ascontiguousarray(np.asarray(L0_attn_Wq, dtype=np.float32)),
        "wk": np.ascontiguousarray(np.asarray(L0_attn_Wk, dtype=np.float32)),
        "wv": np.ascontiguousarray(np.asarray(L0_attn_Wv, dtype=np.float32)),
        "wo": np.ascontiguousarray(np.asarray(L0_attn_Wo, dtype=np.float32)),
        "wup": np.ascontiguousarray(np.asarray(L0_ffn_w_up, dtype=np.float32)),
        "wdn": np.ascontiguousarray(np.asarray(L0_ffn_w_down, dtype=np.float32)),
    }
    nc = build_program()
    in_maps = [dict(ws, x=xs[i]) for i in range(8)]
    res = run_bass_kernel_spmd(nc, in_maps, core_ids=list(range(8)))
    return np.stack([np.asarray(r["out"]) for r in res.results], axis=0)


# version 08
# speedup vs baseline: 1.0143x; 1.0143x over previous
import numpy as np
import concourse.bass as bass
import concourse.mybir as mybir
from concourse.bass_utils import run_bass_kernel_spmd

F32 = mybir.dt.float32
BF16 = mybir.dt.bfloat16
AX = mybir.AxisListType.X
ALU = mybir.AluOpType
AF = mybir.ActivationFunctionType

S = 2048
E = 2048
I = 4096
NS = S // 128
NE = E // 128
NI = I // 128
NG = E // 128
EPS = 1e-5


class Ctr:
    def __init__(self, nc, name):
        self.guard = nc.semaphore(name)
        self.h = self.guard.__enter__()
        self.n = 0


class B:
    ENGINES = ("sync", "scalar", "vector", "tensor", "gpsimd")

    def __init__(self, nc):
        self.nc = nc
        self.streams = {e: [] for e in self.ENGINES}
        self.sems = {}

    def sem(self, name):
        if name not in self.sems:
            self.sems[name] = Ctr(self.nc, name)
        return self.sems[name]

    def waits(self, eng, toks):
        best = {}
        for t in toks:
            if t is None:
                continue
            s, v = t
            if s not in best or best[s] < v:
                best[s] = v
        for s, v in best.items():
            self.streams[eng].append(lambda e, s=s, v=v: e.wait_ge(s.h, v))

    def op(self, eng, fn, waits=(), sem=None):
        self.waits(eng, waits)
        s = self.sem(sem or ("c_" + eng))
        s.n += 1
        v = s.n
        self.streams[eng].append(lambda e, fn=fn, s=s: fn(e).then_inc(s.h, 1))
        return (s, v)

    def dma(self, eng, ch, out, in_, waits=(), **kw):
        self.waits(eng, waits)
        s = self.sem("d_" + ch)
        s.n += 16
        v = s.n
        self.streams[eng].append(
            lambda e, out=out, in_=in_, s=s, kw=kw:
            e.dma_start(out=out, in_=in_, **kw).then_inc(s.h, 16))
        return (s, v)

    def dmat(self, eng, ch, out, in_, waits=()):
        self.waits(eng, waits)
        s = self.sem("d_" + ch)
        s.n += 16
        v = s.n
        self.streams[eng].append(
            lambda e, out=out, in_=in_, s=s:
            e.dma_start_transpose(out=out, in_=in_).then_inc(s.h, 16))
        return (s, v)

    def emit(self):
        with self.nc.Block() as blk:
            for name in self.ENGINES:
                items = self.streams[name]
                if not items:
                    continue
                getattr(blk, name)(lambda e, items=items: [f(e) for f in items])


def build_program(upto=9, debug=()):
    nc = bass.Bass("TRN2", target_bir_lowering=False, debug=False)
    b = B(nc)

    x_d = nc.dram_tensor("x", [S, E], F32, kind="ExternalInput").ap()
    wq_d = nc.dram_tensor("wq", [E, E], F32, kind="ExternalInput").ap()
    wk_d = nc.dram_tensor("wk", [E, E], F32, kind="ExternalInput").ap()
    wv_d = nc.dram_tensor("wv", [E, E], F32, kind="ExternalInput").ap()
    wo_d = nc.dram_tensor("wo", [E, E], F32, kind="ExternalInput").ap()
    wup_d = nc.dram_tensor("wup", [I, E], F32, kind="ExternalInput").ap()
    wdn_d = nc.dram_tensor("wdn", [E, I], F32, kind="ExternalInput").ap()
    out_d = nc.dram_tensor("out", [S, E], F32, kind="ExternalOutput").ap()

    def scratch(name, shape, dtype=BF16):
        kind = "ExternalOutput" if name in debug else "Internal"
        return nc.dram_tensor(name, shape, dtype, kind=kind).ap()

    xn_d = scratch("xn_s", [128, NE, S])
    xq_d = scratch("xq_s", [NS, 128, NE, 128])
    q_d = scratch("q_s", [NE, 128, NS, 128])
    ot_d = scratch("ot_s", [NE, 128, S])
    sat_d = scratch("sat_s", [NS, 128, NG, 128])
    g_d = scratch("g_s", [NS, 128, NG, 128])
    tdt_d = scratch("tdt_s", [NG, 128, NI, 128])
    hq_d = scratch("hq_s", [128, NI, S])

    live = {}

    def push(name, shape, dtype, **kw):
        g = nc.sbuf_tensor(name, shape, dtype, **kw)
        h = g.__enter__()
        live[name] = g
        return h

    def pop(*names):
        for name in names:
            live.pop(name).__exit__(None, None, None)

    def ppush(name, shape):
        g = nc.psum_tensor(name, shape, F32)
        h = g.__enter__()
        live[name] = g
        return h

    st6 = push("st6", [128, 24], F32)
    mv = push("mv", [128, 2], F32)
    sdv = push("sdv", [128, 1], F32)
    rstd = push("rstd", [128, 1], F32)
    amax = push("amax", [128, 1], F32)
    qs_all = push("qs_all", [128, NS], F32)
    qr_all = push("qr_all", [128, NS], F32)
    mrow = push("mrow", [128, 1], F32)
    nmrow = push("nmrow", [128, 1], F32)
    zsum = push("zsum", [128, 1], F32)
    rz_all = push("rz_all", [128, NE], F32)
    wacc_up = push("wacc_up", [128, 1], F32)
    wacc_dn = push("wacc_dn", [128, 1], F32)
    wred = push("wred", [128, 1], F32)
    wred2 = push("wred2", [128, 1], F32)
    ones_f = push("ones_f", [1, 128], F32)
    onescol = push("onescol", [128, 1], F32)
    srow = push("srow", [1, 8], F32)
    sup1 = push("sup1", [1, 1], F32)
    sdn1 = push("sdn1", [1, 1], F32)
    bcast = push("bcast", [128, 8], F32)
    rmax = push("rmax", [128, 2], F32)
    rmx = push("rmx", [128, 1], F32)
    maxh = push("maxh", [128, 1], F32)
    rcph = push("rcph", [128, 1], F32)
    t1 = push("t1", [128, 1], F32)
    u1 = push("u1", [128, 1], F32)
    qs2 = push("qs2", [128, 1], F32)
    cvec = push("cvec", [128, 1], F32)
    cpool = push("cpool", [128, 2], F32)

    pA = ppush("pA", [128, 2048])
    pB = ppush("pB", [128, 2048])

    T = {}

    tk = b.op("gpsimd", lambda e: e.memset(cpool[:, 0:1], EPS))
    tk = b.op("gpsimd", lambda e: e.memset(cpool[:, 1:2], 256.5), waits=[tk])
    nc.const_aps.register(F32, EPS, cpool[:, 0:1])
    nc.const_aps.register(F32, 256.5, cpool[:, 1:2])
    wz_up = b.op("vector", lambda e: e.memset(wacc_up[:], 0.0))
    wz_dn = b.op("vector", lambda e: e.memset(wacc_dn[:], 0.0), waits=[wz_up])
    tk = b.op("gpsimd", lambda e: e.memset(ones_f[:], 1.0), waits=[tk])
    tk = b.op("gpsimd", lambda e: e.memset(onescol[:], 1.0), waits=[tk])
    gp_init = tk

    wup_t = wup_d.rearrange("(t p) e -> t p e", p=128)
    wdn_t = wdn_d.rearrange("(t p) i -> t p i", p=128)
    wstat = [None, None]
    ws_state = {"i": 0, "buf_tok": [None, None], "acc_up": None, "acc_dn": None}

    def stream_wstats(n):
        st_ = ws_state
        for _ in range(n):
            i = st_["i"]
            if i >= 64:
                return
            buf = wstat[i % 2]
            if i < 32:
                src = wup_t[i][:, :]
                acc, red = wacc_up, wred
            else:
                j = i - 32
                src = wdn_t[j // 2][:, (j % 2) * 2048:(j % 2 + 1) * 2048]
                acc, red = wacc_dn, wred2
            ld = b.dma("gpsimd", f"ws{i % 2}", buf[:], src,
                       waits=[st_["buf_tok"][i % 2]])
            key = "acc_up" if i < 32 else "acc_dn"
            init = st_[key] if st_[key] else (wz_up if i < 32 else wz_dn)
            tok = b.op("vector", lambda e, buf=buf, red=red: e.reduce_sum(
                out=red[:], in_=buf[:], axis=AX, apply_absolute_value=True),
                waits=[ld, init])
            st_[key] = b.op("vector", lambda e, acc=acc, red=red: e.tensor_add(
                out=acc[:], in0=acc[:], in1=red[:]), waits=[tok, init])
            st_["buf_tok"][i % 2] = st_[key]
            st_["i"] += 1

    wstat[0] = push("wst0", [128, 2048], F32, side="right")
    wstat[1] = push("wst1", [128, 2048], F32, side="right")
    xin = [push(f"xin{i}", [128, E], F32) for i in range(2)]
    xnf = push("xnf32", [128, E], F32)
    xnb = [push(f"xnb{i}", [128, E], BF16) for i in range(2)]
    xqb = push("xqb", [128, E], BF16)
    xnt = push("xnt", [128, NE, 128], BF16, is_transpose_dest=True)
    xqt = push("xqt", [128, NE, 128], BF16, is_transpose_dest=True)
    ybc = push("ybc", [128, 1024], F32)
    tb = push("tb", [128, 1024], F32)
    ybi = push("ybi", [128, 1024], mybir.dt.int32)
    ccb = push("ccb", [128, 1024], F32)
    ccb2 = push("ccb2", [128, 1024], F32)

    wfQ = push("wfQ", [128, NE, E], BF16)
    xbQ = [push(f"xbQ{i}", [128, NE, 512], BF16) for i in range(2)]
    ebQ = [push(f"ebQ{i}", [128, E], BF16) for i in range(2)]
    qtoks = {}
    wlQ = []
    for cg in range(4):
        wlQ.append(b.dma("gpsimd", f"wf{cg}",
                         wfQ[:, cg * 4:(cg + 1) * 4, :],
                         wq_d[cg * 512:(cg + 1) * 512, :]
                         .rearrange("(c p) f -> p c f", p=128), waits=[]))

    def q_sg(sg):
        xw = [T[("xnS", sg * 4 + 3)]]
        if sg >= 2:
            xw.append(qtoks[("mm", sg - 2, 3)])
        xl = b.dma("sync", f"xb{sg % 2}", xbQ[sg % 2][:],
                   xn_d[:, :, sg * 512:(sg + 1) * 512], waits=xw)
        for stl in range(4):
            st = sg * 4 + stl
            pe = pA if st % 2 == 0 else pB
            first_w = [xl, wlQ[0]]
            if st >= 2:
                first_w.append(qtoks[("ev", st - 2)])
            for ce in range(NE):
                w2 = first_w if ce == 0 else \
                    ([wlQ[ce // 4]] if ce % 4 == 0 else ())
                for fb in range(4):
                    tok = b.op("tensor",
                               lambda e, pe=pe, ce=ce, fb=fb, stl=stl,
                               xb=xbQ[sg % 2]: e.matmul(
                                   pe[:, fb * 512:(fb + 1) * 512],
                                   xb[:, ce, stl * 128:(stl + 1) * 128],
                                   wfQ[:, ce, fb * 512:(fb + 1) * 512],
                                   start=(ce == 0), stop=(ce == NE - 1)),
                               waits=w2 if fb == 0 else ())
                    w2 = ()
            qtoks[("mm", sg, stl)] = tok
            eb = ebQ[st % 2]
            w3 = [tok]
            if st >= 2:
                w3.append(qtoks[("stq", st - 2)])
            ev = b.op("scalar", lambda e, pe=pe, eb=eb: e.activation(
                out=eb[:], in_=pe[:, :], func=AF.Copy), waits=w3)
            qtoks[("ev", st)] = ev
            qtoks[("stq", st)] = b.dma(
                "scalar", f"qs{st % 2}",
                q_d.rearrange("t p c j -> p t c j")[:, :, st, :],
                eb[:].rearrange("p (t j) -> p t j", j=128), waits=[ev])

    x_t = x_d.rearrange("(t p) e -> t p e", p=128)
    for st in range(NS):
        xb = xin[st % 2]
        w = []
        if st >= 2:
            w = [T[("xn", st - 2)], T[("bn", st - 2)]]
        T[("xload", st)] = b.dma("sync", f"x{st % 2}", xb[:], x_t[st], waits=w)
        bns = []
        for c in range(4):
            w = [T[("xload", st)]]
            if c == 0 and st >= 1:
                w.append(T[("bn", st - 1)])
            bns.append(b.op("vector", lambda e, xb=xb, c=c: e.bn_stats(
                out=st6[:, c * 6:(c + 1) * 6], in_=xb[:, c * 512:(c + 1) * 512]),
                waits=w))
        w = list(bns)
        if st >= 1:
            w += [T[("sq", st - 1)], T[("xn", st - 1)]]
        T[("bn", st)] = b.op("vector", lambda e: e.bn_aggr(
            out=mv[:], in_=st6[:]), waits=w)
        T[("sq", st)] = b.op("scalar", lambda e: e.activation(
            out=sdv[:], in_=mv[:, 1:2], func=AF.Sqrt, bias=EPS),
            waits=[T[("bn", st)]] + ([gp_init] if st == 0 else []))
        w = [T[("sq", st)]]
        if st >= 1:
            w += [T[("xn", st - 1)], T[("qs", st - 1)]]
        T[("rstd", st)] = b.op("vector", lambda e: e.reciprocal(
            out=rstd[:], in_=sdv[:]), waits=w)
        xnt_b = xnb[st % 2]
        w = [T[("rstd", st)]]
        if st >= 1:
            w += [T[("absmax", st - 1)], T[("y", st - 1, 1)],
                  T[("xnb", st - 1)]]
        T[("xn", st)] = b.op("vector", lambda e, xb=xb:
                             e.tensor_scalar(
                                 out=xnf[:], in0=xb[:], scalar1=mv[:, 0:1],
                                 scalar2=rstd[:], op0=ALU.subtract,
                                 op1=ALU.mult), waits=w)
        w = [T[("xn", st)]]
        if st >= 2:
            w.append(T[("xnT", st - 2)])
        T[("xnb", st)] = b.op("vector", lambda e, xnt_b=xnt_b: e.tensor_copy(
            out=xnt_b[:], in_=xnf[:]), waits=w)
        T[("absmax", st)] = b.op("vector", lambda e: e.reduce_max(
            out=amax[:], in_=xnf[:], axis=AX, apply_absolute_value=True),
            waits=[T[("xn", st)]])
        tok = b.op("vector", lambda e: e.tensor_scalar_max(
            out=amax[:], in0=amax[:], scalar1=EPS), waits=[T[("absmax", st)]])
        tok = b.op("vector", lambda e, st=st: e.tensor_scalar_mul(
            out=qr_all[:, st:st + 1], in0=amax[:], scalar1=1.0 / 127.0),
            waits=[tok])
        tok2 = b.op("vector", lambda e: e.reciprocal(out=rstd[:], in_=amax[:]),
                    waits=[tok, T[("xn", st)]])
        T[("qs", st)] = b.op("vector", lambda e, st=st: e.tensor_scalar_mul(
            out=qs_all[:, st:st + 1], in0=rstd[:], scalar1=127.0), waits=[tok2])
        prev = None
        for ch in range(2):
            sl = slice(ch * 1024, (ch + 1) * 1024)
            w = [T[("qs", st)]]
            if prev is not None:
                w.append(prev)
            elif st >= 1:
                w.append(T[("xq", st - 1, 1)])
            ty = b.op("scalar",
                lambda e, st=st, sl=sl: e.activation(
                    out=ybc[:], in_=xnf[:, sl], func=AF.Copy,
                    scale=qs_all[:, st:st + 1]), waits=w)
            T[("y", st, ch)] = ty
            tyi = b.op("vector", lambda e: e.tensor_copy(
                out=ybi[:], in_=ybc[:]), waits=[ty])
            td = b.op("vector", lambda e: e.tensor_sub(
                out=tb[:], in0=ybc[:], in1=ybi[:]), waits=[tyi])
            tca = b.op("vector", lambda e: e.tensor_single_scalar(
                out=ccb[:], in_=tb[:], scalar=0.5, op=ALU.is_gt), waits=[td])
            tcb = b.op("vector", lambda e: e.scalar_tensor_tensor(
                out=ybc[:], in0=tb[:], scalar=1.5, in1=ccb[:],
                op0=ALU.is_gt, op1=ALU.add), waits=[tca])
            tcc = b.op("vector", lambda e: e.tensor_scalar(
                out=ccb[:], in0=tb[:], scalar1=-0.5, scalar2=-1.0,
                op0=ALU.is_lt, op1=ALU.mult), waits=[tcb])
            tcd = b.op("vector", lambda e: e.scalar_tensor_tensor(
                out=ccb2[:], in0=tb[:], scalar=-1.5, in1=ccb[:],
                op0=ALU.is_lt, op1=ALU.subtract), waits=[tcc])
            tce = b.op("vector", lambda e: e.tensor_sub(
                out=tb[:], in0=ybc[:], in1=ccb2[:]), waits=[tcd])
            wq_ = [tce]
            if ch == 0 and st >= 1:
                wq_.append(T[("xqT", st - 1)])
            T[("xq", st, ch)] = b.op("vector",
                lambda e, sl=sl: e.tensor_add(
                    out=xqb[:, sl], in0=ybi[:], in1=tb[:]), waits=wq_)
            prev = T[("xq", st, ch)]
        w = [T[("xnb", st)]]
        if st >= 1:
            w.append(T[("xnS", st - 1)])
        T[("xnT", st)] = b.dmat("sync", "xnT", xnt[:], xnt_b[:], waits=w)
        T[("xnS", st)] = b.dma("scalar", "xnS",
                               xn_d[:, :, st * 128:(st + 1) * 128], xnt[:],
                               waits=[T[("xnT", st)]])
        w = [T[("xq", st, 1)]]
        if st >= 1:
            w.append(T[("xqS", st - 1)])
        T[("xqT", st)] = b.dmat("sync", "xqT", xqt[:], xqb[:], waits=w)
        T[("xqS", st)] = b.dma("scalar", "xqS", xq_d[st], xqt[:],
                               waits=[T[("xqT", st)]])
        stream_wstats(1)
        if st % 4 == 3:
            q_sg(st // 4)

    xn_done = T[("xnS", NS - 1)]
    ph1_done = [T[("bn", NS - 1)], T[("absmax", NS - 1)],
                T[("xq", NS - 1, 1)], T[("xnS", NS - 1)], T[("xqS", NS - 1)],
                T[("sq", NS - 1)], T[("qs", NS - 1)], T[("xn", NS - 1)],
                T[("xnb", NS - 1)], T[("y", NS - 1, 1)]]
    if upto < 2:
        b.emit()
        return nc

    def proj_tm(w_dram, phase, dst_sb, psum_last, prev_done, xn_tok=None):
        wf = push(f"wf{phase}", [128, NE, E], BF16)
        xblk = [push(f"xb{phase}{i}", [128, NE, 512], BF16) for i in range(2)]
        ebv = [push(f"eb{phase}{i}", [128, E], BF16) for i in range(2)] \
            if dst_sb is None else None
        toks = {}
        wl = []
        for cg in range(4):
            wl.append(b.dma("gpsimd", f"wf{cg}",
                            wf[:, cg * 4:(cg + 1) * 4, :],
                            w_dram[cg * 512:(cg + 1) * 512, :]
                            .rearrange("(c p) f -> p c f", p=128),
                            waits=list(prev_done)))
        for sg in range(4):
            xw = [xn_tok(sg) if xn_tok else xn_done] + \
                (list(prev_done) if sg < 2 else [])
            if sg >= 2:
                xw.append(toks[("mm", sg - 2, 3)])
            xl = b.dma("sync", f"xb{sg % 2}",
                       xblk[sg % 2][:],
                       xn_d[:, :, sg * 512:(sg + 1) * 512], waits=xw)
            for stl in range(4):
                st = sg * 4 + stl
                pe = pA if st % 2 == 0 else pB
                first_w = [xl, wl[0]]
                if st >= 2:
                    first_w.append(toks[("ev", st - 2)])
                elif psum_last[st % 2] is not None:
                    first_w.append(psum_last[st % 2])
                for ce in range(NE):
                    w2 = first_w if ce == 0 else \
                        ([wl[ce // 4]] if ce % 4 == 0 else ())
                    for fb in range(4):
                        tok = b.op("tensor",
                                   lambda e, pe=pe, ce=ce, fb=fb, stl=stl,
                                   xb=xblk[sg % 2]: e.matmul(
                                       pe[:, fb * 512:(fb + 1) * 512],
                                       xb[:, ce, stl * 128:(stl + 1) * 128],
                                       wf[:, ce, fb * 512:(fb + 1) * 512],
                                       start=(ce == 0), stop=(ce == NE - 1)),
                                   waits=w2 if fb == 0 else ())
                        w2 = ()
                toks[("mm", sg, stl)] = tok
                if dst_sb is None:
                    eb = ebv[st % 2]
                    w3 = [tok]
                    if st >= 2:
                        w3.append(toks[("stq", st - 2)])
                    ev = b.op("scalar", lambda e, pe=pe, eb=eb: e.activation(
                        out=eb[:], in_=pe[:, :], func=AF.Copy), waits=w3)
                    toks[("ev", st)] = ev
                    toks[("stq", st)] = b.dma(
                        "scalar", f"qs{st % 2}",
                        q_d.rearrange("t p c j -> p t c j")[:, :, st, :],
                        eb[:].rearrange("p (t j) -> p t j", j=128),
                        waits=[ev])
                else:
                    ev = b.op("scalar", lambda e, pe=pe, st=st: e.activation(
                        out=dst_sb[:, st, :], in_=pe[:, :], func=AF.Copy),
                        waits=[tok])
                    toks[("ev", st)] = ev
            stream_wstats(6)
        res_last = {0: toks[("ev", NS - 2)], 1: toks[("ev", NS - 1)]}
        mm_last = toks[("mm", 3, 3)]
        if dst_sb is None:
            res_done = toks[("stq", NS - 1)]
            extra = [toks[("stq", NS - 2)]]
            pop(f"eb{phase}1", f"eb{phase}0", f"xb{phase}1", f"xb{phase}0",
                f"wf{phase}")
        else:
            res_done = toks[("ev", NS - 1)]
            extra = []
            pop(f"xb{phase}1", f"xb{phase}0", f"wf{phase}")
        return res_done, res_last, mm_last, extra

    q_done = qtoks[("stq", NS - 1)]
    q_extra = [qtoks[("stq", NS - 2)]]
    q_mm_last = qtoks[("mm", 3, 3)]
    plast = {0: qtoks[("ev", NS - 2)], 1: qtoks[("ev", NS - 1)]}
    pop("ebQ1", "ebQ0", "xbQ1", "xbQ0", "wfQ")
    pop("ccb2", "ccb", "ybi", "tb", "ybc", "xqt", "xnt", "xqb", "xnb1",
        "xnb0", "xnf32", "xin1", "xin0")
    K_sb = push("K_sb", [128, NS, E], BF16)
    k_done, plast, k_mm_last, _ = proj_tm(
        wk_d, "K", K_sb, plast, [q_mm_last, q_done] + q_extra + ph1_done)
    stream_wstats(64)
    ws_done = [ws_state["acc_up"], ws_state["acc_dn"]]
    pop("wst1", "wst0")

    if upto < 4:
        b.emit()
        return nc

    V_sb = push("V_sb", [128, NE, S], BF16)
    XNf = push("XNf", [128, NE, E], BF16)
    wcol = [push(f"wcol{i}", [128, NE, 128], BF16) for i in range(2)]
    ph4_prev = [k_mm_last, k_done, q_done] + q_extra + ws_done
    xnl = b.dma("sync", "xnf", XNf[:], xn_d[:],
                waits=[xn_done] + ph4_prev)
    vtoks = {}
    for ft in range(NE):
        buf = wcol[ft % 2]
        w = [gp_init] + (ph4_prev if ft < 2 else [])
        if ft >= 2:
            w.append(vtoks[("mm", ft - 2, 3)])
        ldw = b.dma("gpsimd", f"wc{ft % 2}", buf[:],
                    wv_d[:, ft * 128:(ft + 1) * 128]
                    .rearrange("(c p) f -> p c f", p=128), waits=w)
        pe = pA if ft % 2 == 0 else pB
        for sb_ in range(4):
            w2 = [ldw, xnl]
            if ft >= 2:
                w2.append(vtoks[("ev", ft - 2)])
            elif plast[ft % 2] is not None:
                w2.append(plast[ft % 2])
            for ce in range(NE):
                tok = b.op("tensor", lambda e, pe=pe, ce=ce, sb_=sb_, buf=buf:
                           e.matmul(pe[:, sb_ * 512:(sb_ + 1) * 512],
                                    buf[:, ce, :],
                                    XNf[:, ce, sb_ * 512:(sb_ + 1) * 512],
                                    start=(ce == 0), stop=(ce == NE - 1)),
                           waits=w2 if ce == 0 else ())
            vtoks[("mm", ft, sb_)] = tok
        ev = b.op("scalar", lambda e, pe=pe, ft=ft: e.activation(
            out=V_sb[:, ft, :], in_=pe[:, :], func=AF.Copy),
            waits=[vtoks[("mm", ft, 3)]])
        vtoks[("ev", ft)] = ev
    v_done = vtoks[("ev", NE - 1)]
    v_plast = {0: vtoks[("ev", NE - 2)], 1: vtoks[("ev", NE - 1)]}
    v_mm_last = vtoks[("mm", NE - 1, 3)]
    pop("wcol1", "wcol0", "XNf")

    if upto < 5:
        b.emit()
        return nc

    qcol = [push(f"qcol{i}", [128, NS, 128], BF16) for i in range(2)]
    ztile = [push(f"zt{i}", [128, E], BF16) for i in range(2)]
    ztp = [push(f"ztp{i}", [128, NE, 128], BF16, is_transpose_dest=True)
           for i in range(2)]
    otile = [push(f"oti{i}", [128, E], BF16) for i in range(2)]
    mrow2 = push("mrow2", [128, 1], F32)
    zsa = push("zsa", [128, 1], F32)
    zsb = push("zsb", [128, 1], F32)
    ASET = [pA[:, 0:1024], pA[:, 1024:2048], pB[:, 0:1024]]
    ph5 = {}

    def emit_O(et):
        zp = ztp[et % 2]
        ot = otile[et % 2]
        for sb_ in range(4):
            pe = pB[:, 1024 + (sb_ % 2) * 512:1536 + (sb_ % 2) * 512]
            w2 = [ph5[("zT", et)]]
            if et == 0 and sb_ < 2:
                w2.append(v_plast[1])
            if sb_ >= 2:
                w2.append(ph5[("oev", et, sb_ - 2)])
            elif et >= 1:
                w2.append(ph5[("oev", et - 1, sb_ + 2)])
            for cf in range(NE):
                tok = b.op("tensor", lambda e, pe=pe, cf=cf, sb_=sb_, zp=zp:
                           e.matmul(pe, zp[:, cf, :],
                                    V_sb[:, cf, sb_ * 512:(sb_ + 1) * 512],
                                    start=(cf == 0), stop=(cf == NE - 1)),
                           waits=w2 if cf == 0 else ())
            ph5[("Omm", et, sb_)] = tok
            w3 = [tok, ph5[("rz", et)]]
            if et >= 2:
                w3.append(ph5[("ost", et - 2)])
            ph5[("oev", et, sb_)] = b.op(
                "scalar", lambda e, pe=pe, ot=ot, sb_=sb_, et=et: e.activation(
                    out=ot[:, sb_ * 512:(sb_ + 1) * 512], in_=pe, func=AF.Copy,
                    scale=rz_all[:, et:et + 1]), waits=w3)
        ph5[("ost", et)] = b.dma("scalar", f"os{et % 2}", ot_d[et],
                                 otile[et % 2][:],
                                 waits=[ph5[("oev", et, s_)] for s_ in range(4)])

    for et in range(NE):
        qc = qcol[et % 2]
        w = [q_done]
        if et < 2:
            w.append(v_mm_last)
        if et >= 2:
            w.append(ph5[("Amm", et - 2, 1)])
        ph5[("qld", et)] = b.dma("sync", f"qc{et % 2}", qc[:], q_d[et],
                                 waits=w)
        for h in range(2):
            k = 2 * et + h
            pe = ASET[k % 3]
            w = [ph5[("qld", et)]]
            if et == 0 and h == 0:
                w += [k_done, v_done, v_plast[0]]
            if k == 2:
                w.append(v_plast[1]) if False else None
                w += [k_done, v_done, v_plast[1]]
            if k >= 3:
                kp = k - 3
                w.append(ph5[("exph", kp // 2, kp % 2)])
            first = True
            for cs in range(NS):
                for fb in (0, 1):
                    f0 = (2 * h + fb) * 512
                    tok = b.op("tensor",
                               lambda e, pe=pe, cs=cs, f0=f0, qc=qc: e.matmul(
                                   pe[:, (f0 % 1024):(f0 % 1024) + 512],
                                   qc[:, cs, :],
                                   K_sb[:, cs, f0:f0 + 512],
                                   start=(cs == 0), stop=(cs == NS - 1)),
                               waits=w if first else ())
                    first = False
            ph5[("Amm", et, h)] = tok
            mxb = mrow if h == 0 else mrow2
            w = [tok]
            if et >= 1:
                w.append(ph5[("nm", et - 1)])
            ph5[("mxh", et, h)] = b.op("vector",
                lambda e, pe=pe, mxb=mxb: e.reduce_max(
                    out=mxb[:], in_=pe, axis=AX), waits=w)
        mxc = b.op("vector", lambda e: e.tensor_max(
            out=mrow[:], in0=mrow[:], in1=mrow2[:]),
            waits=[ph5[("mxh", et, 0)], ph5[("mxh", et, 1)]])
        w = [mxc]
        if et >= 1:
            w.append(ph5[("exph", et - 1, 1)])
        ph5[("nm", et)] = b.op("scalar", lambda e: e.activation(
            out=nmrow[:], in_=mrow[:], func=AF.Copy, scale=-0.125), waits=w)
        zt = ztile[et % 2]
        for h in range(2):
            pe = ASET[(2 * et + h) % 3]
            zs = zsa if h == 0 else zsb
            w = [ph5[("nm", et)]]
            if et >= 2 and h == 0:
                w.append(ph5[("zT", et - 2)])
            if et >= 1:
                w.append(ph5[("rz", et - 1)])
            ph5[("exph", et, h)] = b.op("scalar",
                lambda e, pe=pe, zt=zt, h=h, zs=zs: e.activation(
                    out=zt[:, h * 1024:(h + 1) * 1024], in_=pe, func=AF.Exp,
                    bias=nmrow[:], scale=0.125, accum_out=zs[:]), waits=w)
        zadd = b.op("vector", lambda e: e.tensor_add(
            out=zsa[:], in0=zsa[:], in1=zsb[:]),
            waits=[ph5[("exph", et, 0)], ph5[("exph", et, 1)]])
        ph5[("rz", et)] = b.op("vector", lambda e, et=et: e.reciprocal(
            out=rz_all[:, et:et + 1], in_=zsa[:]), waits=[zadd])
        w = [ph5[("exph", et, 0)], ph5[("exph", et, 1)]]
        if et >= 2:
            w.append(ph5[("Omm", et - 2, 3)])
        ph5[("zT", et)] = b.dmat("sync", f"zt{et % 2}", ztp[et % 2][:], zt[:],
                                 waits=w)
        if et >= 1:
            emit_O(et - 1)
    emit_O(NE - 1)
    ph5_done = [ph5[("Omm", NE - 1, 3)], ph5[("oev", NE - 1, 3)],
                ph5[("oev", NE - 1, 2)], ph5[("exph", NE - 1, 0)],
                ph5[("exph", NE - 1, 1)], ph5[("exph", NE - 2, 1)],
                ph5[("rz", NE - 1)], ph5[("Amm", NE - 1, 1)],
                ph5[("zT", NE - 1)], ph5[("zT", NE - 2)],
                ph5[("ost", NE - 1)], ph5[("ost", NE - 2)],
                ph5[("nm", NE - 1)],
                ph5[("oev", NE - 1, 0)], ph5[("oev", NE - 1, 1)]]
    pop("zsb", "zsa", "mrow2", "oti1", "oti0", "ztp1", "ztp0", "zt1",
        "zt0", "qcol1", "qcol0")
    pop("V_sb", "K_sb")

    if upto < 6:
        b.emit()
        return nc

    OTh = push("OTh", [128, NE, 1024], BF16)
    stile = [push(f"sti{i}", [128, 1024], BF16) for i in range(2)]
    wcol6 = [push(f"wc6{i}", [128, NE, 128], BF16) for i in range(2)]

    allw = [ws_state["acc_up"], ws_state["acc_dn"]]
    ph5_all = ph5_done + [ph5[("oev", NE - 1, s_)] for s_ in range(4)]
    mm1 = b.op("tensor", lambda e: e.matmul(
        pB[0:1, 1536:1537], onescol[:], wacc_up[:], start=True, stop=True),
        waits=allw + [gp_init] + ph5_all)
    mm2 = b.op("tensor", lambda e: e.matmul(
        pB[0:1, 1544:1545], onescol[:], wacc_dn[:], start=True, stop=True),
        waits=[])
    t_a = b.op("vector", lambda e: e.tensor_scalar(
        out=sup1[:], in0=pB[0:1, 1536:1537], scalar1=1.0 / (I * E),
        scalar2=EPS, op0=ALU.mult, op1=ALU.max), waits=[mm1])
    t_b = b.op("vector", lambda e: e.tensor_scalar(
        out=sdn1[:], in0=pB[0:1, 1544:1545], scalar1=1.0 / (I * E),
        scalar2=EPS, op0=ALU.mult, op1=ALU.max), waits=[mm2])
    packs = [
        b.op("vector", lambda e: e.tensor_scalar_mul(
            out=srow[:, 0:1], in0=sup1[:], scalar1=0.5), waits=[t_a]),
        b.op("vector", lambda e: e.tensor_scalar_mul(
            out=srow[:, 1:2], in0=sup1[:], scalar1=-0.5), waits=[t_a]),
        b.op("vector", lambda e: e.tensor_scalar_mul(
            out=srow[:, 2:3], in0=sdn1[:], scalar1=0.5), waits=[t_b]),
        b.op("vector", lambda e: e.tensor_scalar_mul(
            out=srow[:, 3:4], in0=sdn1[:], scalar1=-0.5), waits=[t_b]),
        b.op("vector", lambda e: e.tensor_scalar_mul(
            out=srow[:, 4:5], in0=sup1[:], scalar1=1.0), waits=[t_a]),
        b.op("vector", lambda e: e.tensor_scalar_mul(
            out=srow[:, 5:6], in0=sdn1[:], scalar1=1.0), waits=[t_b]),
        b.op("vector", lambda e: e.memset(srow[:, 6:8], 0.0)),
    ]
    mm3 = b.op("tensor", lambda e: e.matmul(
        pB[:, 1552:1560], ones_f[:], srow[:], start=True, stop=True),
        waits=packs)
    bctok = b.op("vector", lambda e: e.tensor_copy(
        out=bcast[:], in_=pB[:, 1552:1560]), waits=[mm3])

    TUA = push("TUA", [128, NE, E], BF16, is_transpose_dest=True, side="right")
    TUB = push("TUB", [128, NE, E], BF16, is_transpose_dest=True, side="right")
    wstat[0] = push("wst0b", [128, 2048], F32, side="right")
    wstat[1] = push("wst1b", [128, 2048], F32, side="right")
    mskc = push("mskc", [128, 2048], BF16)
    ttile = [push(f"tt{i}", [128, 2048], BF16) for i in range(2)]
    tstg = push("tstg", [128, 16, 128], BF16, is_transpose_dest=True)
    ph7 = {}

    def ph7_unit(u):
        if u < NI:
            ic = u
            buf = wstat[ic % 2]
            w = [bctok]
            if ic >= 2:
                w.append(ph7[("tern", ic - 2)])
            ld = b.dma("sync", f"tu{ic % 2}", buf[:], wup_t[ic][:, :],
                       waits=w)
            w = [ld]
            if ic >= 1:
                w.append(ph7[("tern", ic - 1)])
            msk = b.op("vector", lambda e, buf=buf: e.tensor_single_scalar(
                out=mskc[:], in_=buf[:], scalar=bcast[:, 1:2], op=ALU.is_le),
                waits=w)
            tt = ttile[ic % 2]
            w2 = [msk]
            if ic >= 2:
                w2.append(ph7[("tT", ic - 2)])
            ph7[("tern", ic)] = b.op("vector", lambda e, buf=buf, tt=tt:
                e.scalar_tensor_tensor(
                    out=tt[:], in0=buf[:], scalar=bcast[:, 0:1],
                    in1=mskc[:], op0=ALU.is_ge, op1=ALU.subtract), waits=w2)
            dst = TUA if ic < 16 else TUB
            j = ic % 16
            ph7[("tT", ic)] = b.dmat("sync", f"tT{ic % 2}",
                                     dst[:, :, j * 128:(j + 1) * 128],
                                     tt[:], waits=[ph7[("tern", ic)]])
        else:
            k = u - NI
            gc, h = k // 2, k % 2
            buf = wstat[k % 2]
            w = [ph7[("tern", NI - 2 + k)] if k < 2 else ph7[("dtern", k - 2)]]
            ld = b.dma("gpsimd", f"tw{k % 2}", buf[:],
                       wdn_t[gc][:, h * 2048:(h + 1) * 2048], waits=w)
            w = [ld, ph7[("tern", NI - 1)] if k == 0 else ph7[("dtern", k - 1)]]
            msk = b.op("vector", lambda e, buf=buf: e.tensor_single_scalar(
                out=mskc[:], in_=buf[:], scalar=bcast[:, 3:4], op=ALU.is_le),
                waits=w)
            tt = ttile[k % 2]
            w2 = [msk]
            if k >= 2:
                w2.append(ph7[("dT", k - 2)])
            else:
                w2.append(ph7[("tT", NI - 2 + k)])
            ph7[("dtern", k)] = b.op("vector", lambda e, buf=buf, tt=tt:
                e.scalar_tensor_tensor(
                    out=tt[:], in0=buf[:], scalar=bcast[:, 2:3],
                    in1=mskc[:], op0=ALU.is_ge, op1=ALU.subtract), waits=w2)
            w3 = [ph7[("dtern", k)]]
            if k >= 1:
                w3.append(ph7[("dst", k - 1)])
            tT = b.dmat("sync", "dT", tstg[:], tt[:], waits=w3)
            ph7[("dT", k)] = tT
            ph7[("dst", k)] = b.dma(
                "gpsimd", "dst", tdt_d[gc][:, h * 16:(h + 1) * 16, :],
                tstg[:], waits=[tT])

    ph6 = {}
    units = iter(range(NI + NG * 2))
    for hf in range(2):
        olds = []
        for ce in range(NE):
            w = [ph5[("ost", ce)]] + (list(ph5_done) if hf == 0 else [])
            if hf == 1:
                w.append(ph6[("mm", 0, NG - 1, 1)])
            olds.append(b.dma("sync", "ol", OTh[:, ce, :],
                              ot_d[ce][:, hf * 1024:(hf + 1) * 1024], waits=w))
        for gt in range(NG):
            buf6 = wcol6[gt % 2]
            w = list(olds)
            if gt >= 2:
                w.append(ph6[("mm", hf, gt - 2, 1)])
            elif hf == 1:
                w.append(ph6[("mm", 0, NG - 2 + gt, 1)])
            ldw = b.dma("gpsimd", f"w6{gt % 2}", buf6[:],
                        wo_d[:, gt * 128:(gt + 1) * 128]
                        .rearrange("(c p) f -> p c f", p=128), waits=w)
            for sb_ in range(2):
                pe = pA if gt % 2 == 0 else pB
                w2 = list(olds) + [ldw]
                if gt >= 2:
                    w2.append(ph6[("sev", hf, gt - 2)])
                elif hf == 0 and gt < 2:
                    w2 += [ph5[("oev", NE - 1, s_)] for s_ in range(4)]
                    w2 += [ph5[("exph", NE - 1, 0)], ph5[("exph", NE - 1, 1)],
                           ph5[("exph", NE - 2, 1)]]
                elif hf == 1:
                    w2.append(ph6[("sev", 0, NG - 2 + gt)])
                for ce in range(NE):
                    tok = b.op("tensor",
                               lambda e, pe=pe, ce=ce, sb_=sb_, buf6=buf6:
                               e.matmul(pe[:, sb_ * 512:(sb_ + 1) * 512],
                                        buf6[:, ce, :],
                                        OTh[:, ce, sb_ * 512:(sb_ + 1) * 512],
                                        start=(ce == 0), stop=(ce == NE - 1)),
                               waits=w2 if ce == 0 else ())
                ph6[("mm", hf, gt, sb_)] = tok
            stl = stile[gt % 2]
            w3 = [ph6[("mm", hf, gt, 1)], ph6[("mm", hf, gt, 0)]]
            if gt >= 2:
                w3.append(ph6[("sst", hf, gt - 2)])
            elif hf == 1:
                w3.append(ph6[("sst", 0, NG - 2 + gt)])
            ph6[("sev", hf, gt)] = b.op(
                "scalar", lambda e, pe=pe, stl=stl: e.activation(
                    out=stl[:], in_=pe[:, 0:1024], func=AF.Copy), waits=w3)
            ph6[("sst", hf, gt)] = b.dma(
                "scalar", f"ss{gt % 2}",
                sat_d[hf * 8:(hf + 1) * 8]
                .rearrange("t p c j -> p t c j")[:, :, gt, :],
                stl[:].rearrange("p (t j) -> p t j", j=128),
                waits=[ph6[("sev", hf, gt)]])
            for _ in range(2):
                u = next(units, None)
                if u is not None:
                    ph7_unit(u)
    for u in units:
        ph7_unit(u)
    satall = [ph6[("sst", 1, NG - 1)], ph6[("sst", 1, NG - 2)]]
    ph6_last_mm = ph6[("mm", 1, NG - 1, 1)]
    ph6_lastev = {0: ph6[("sev", 1, NG - 2)], 1: ph6[("sev", 1, NG - 1)]}
    tup_done = [ph7[("tT", NI - 1)], ph7[("tT", NI - 2)]]
    tdn_done = ph7[("dst", NG * 2 - 1)]
    ph7_done = [tdn_done, ph7[("dT", NG * 2 - 1)],
                ph7[("dtern", NG * 2 - 1)], ph7[("dtern", NG * 2 - 2)],
                ph7[("dst", NG * 2 - 2)]] + tup_done
    pop("tstg", "tt1", "tt0", "mskc")
    pop("wst1b", "wst0b")
    pop("wc61", "wc60", "sti1", "sti0", "OTh")

    if upto < 8:
        b.emit()
        return nc

    xqc = [push(f"xqc{i}", [128, NE, 128], BF16) for i in range(2)]
    hqb = push("hqb", [128, I], BF16)
    hqs = push("hqs", [128, NI, 128], BF16, is_transpose_dest=True)
    rbufs = [push(f"rbuf{i}", [128, I], F32) for i in range(2)]
    tbuf = push("tbuf", [128, 1024], F32)
    dbuf = push("dbuf", [128, 1024], F32)
    hibuf = push("hibuf", [128, 1024], mybir.dt.int32)
    c2buf = push("c2buf", [128, 1024], F32)
    sgnb = push("sgnb", [128, 1024], F32)
    ph8 = {}

    def emit_H(st):
        rbuf = rbufs[st % 2]
        xc = xqc[st % 2]
        w = [T[("xqS", st)]]
        if st < 2:
            w += ph7_done + [ph6_last_mm] + satall
        if st >= 2:
            w.append(ph8[("hmm", st - 2, 1)])
        ph8[("xqld", st)] = b.dma("sync", f"xq{st % 2}", xc[:], xq_d[st],
                                  waits=w)
        for ih in range(2):
            pe = pA if ih == 0 else pB
            TU = TUA if ih == 0 else TUB
            w2 = [ph8[("xqld", st)]]
            if st >= 1:
                w2.append(ph8[("relu", st - 1, ih)])
            else:
                w2.append(bctok)
                w2.append(ph6_lastev[ih])
            first = True
            for ib in range(4):
                for ce in range(NE):
                    tok = b.op("tensor",
                               lambda e, pe=pe, ib=ib, ce=ce, xc=xc, TU=TU:
                               e.matmul(pe[:, ib * 512:(ib + 1) * 512],
                                        xc[:, ce, :],
                                        TU[:, ce, ib * 512:(ib + 1) * 512],
                                        start=(ce == 0), stop=(ce == NE - 1)),
                               waits=w2 if first else ())
                    first = False
            ph8[("hmm", st, ih)] = tok
            w3 = [tok]
            if st >= 2:
                w3.append(ph8[("lastt", st - 2)])
                w3.append(ph8[("r01", st - 2)])
            ph8[("relu", st, ih)] = b.op(
                "scalar", lambda e, pe=pe, ih=ih, rbuf=rbuf: e.activation(
                    out=rbuf[:, ih * 2048:(ih + 1) * 2048], in_=pe[:, :],
                    func=AF.Relu), waits=w3)

    def emit_quant(st):
        rbuf = rbufs[st % 2]
        r0 = b.op("vector", lambda e, rbuf=rbuf: e.reduce_max(
            out=rmax[:, 0:1], in_=rbuf[:, 0:2048], axis=AX),
            waits=[ph8[("relu", st, 0)]] +
                  ([ph8[("t1", st - 1)]] if st >= 1 else []))
        r1 = b.op("vector", lambda e, rbuf=rbuf: e.reduce_max(
            out=rmax[:, 1:2], in_=rbuf[:, 2048:4096], axis=AX),
            waits=[ph8[("relu", st, 1)]])
        rm = b.op("vector", lambda e: e.tensor_max(
            out=rmx[:], in0=rmax[:, 0:1], in1=rmax[:, 1:2]), waits=[r0, r1])
        ph8[("r01", st)] = rm
        w = [rm]
        if st >= 1:
            w.append(ph8[("qs2", st - 1)])
        ph8[("t1", st)] = b.op("vector", lambda e, st=st: e.tensor_mul(
            out=t1[:], in0=bcast[:, 4:5], in1=qr_all[:, st:st + 1]), waits=w)
        tok = b.op("vector", lambda e: e.tensor_mul(
            out=maxh[:], in0=t1[:], in1=rmx[:]),
            waits=[ph8[("t1", st)]] +
                  ([ph8[("cv", st - 1)]] if st >= 1 else []))
        tok = b.op("vector", lambda e: e.tensor_mul(
            out=maxh[:], in0=maxh[:], in1=maxh[:]), waits=[tok])
        tok = b.op("vector", lambda e: e.tensor_scalar_max(
            out=maxh[:], in0=maxh[:], scalar1=EPS), waits=[tok])
        tok_r = b.op("vector", lambda e: e.reciprocal(
            out=rcph[:], in_=maxh[:]), waits=[tok])
        tok = b.op("vector", lambda e: e.tensor_mul(
            out=u1[:], in0=t1[:], in1=t1[:]), waits=[tok_r])
        ph8[("qs2", st)] = b.op("vector", lambda e: e.tensor_scalar(
            out=qs2[:], in0=u1[:], scalar1=rcph[:], scalar2=127.0,
            op0=ALU.mult, op1=ALU.mult), waits=[tok])
        ph8[("cv", st)] = b.op("vector", lambda e: e.tensor_scalar(
            out=cvec[:], in0=maxh[:], scalar1=bcast[:, 5:6],
            scalar2=1.0 / 127.0, op0=ALU.mult, op1=ALU.mult), waits=[tok_r])
        hq_parts = []
        for ch in range(4):
            sl = slice(ch * 1024, (ch + 1) * 1024)
            w4 = [ph8[("qs2", st)]] if ch == 0 else [hq_parts[-1]]
            if ch == 0 and st >= 1:
                w4.append(ph8[("hqT", st - 1)])
            t_t = b.op("vector", lambda e, sl=sl, rbuf=rbuf:
                       e.scalar_tensor_tensor(
                           out=tbuf[:], in0=rbuf[:, sl], scalar=qs2[:],
                           in1=rbuf[:, sl], op0=ALU.mult, op1=ALU.mult),
                       waits=w4)
            t_yi = b.op("vector", lambda e: e.tensor_copy(
                out=hibuf[:], in_=tbuf[:]), waits=[t_t])
            t_d = b.op("vector", lambda e: e.tensor_sub(
                out=dbuf[:], in0=tbuf[:], in1=hibuf[:]), waits=[t_yi])
            t_a = b.op("scalar", lambda e: e.activation(
                out=tbuf[:], in_=dbuf[:], func=AF.Abs), waits=[t_d])
            t_s = b.op("scalar", lambda e: e.activation(
                out=sgnb[:], in_=dbuf[:], func=AF.Sign), waits=[t_a])
            t_w1 = b.op("vector", lambda e: e.tensor_single_scalar(
                out=c2buf[:], in_=tbuf[:], scalar=0.5, op=ALU.is_gt),
                waits=[t_a])
            t_w2 = b.op("vector", lambda e: e.scalar_tensor_tensor(
                out=dbuf[:], in0=tbuf[:], scalar=1.5, in1=c2buf[:],
                op0=ALU.is_gt, op1=ALU.add), waits=[t_w1, t_s])
            t_c = b.op("vector", lambda e: e.tensor_mul(
                out=tbuf[:], in0=sgnb[:], in1=dbuf[:]), waits=[t_w2])
            t_cp = b.op("vector", lambda e: e.tensor_single_scalar(
                out=c2buf[:], in_=tbuf[:], scalar=cvec[:], op=ALU.mult),
                waits=[t_c, ph8[("cv", st)]])
            t_hq = b.op("vector", lambda e, sl=sl: e.scalar_tensor_tensor(
                out=hqb[:, sl], in0=hibuf[:], scalar=cvec[:], in1=c2buf[:],
                op0=ALU.mult, op1=ALU.add), waits=[t_cp])
            hq_parts.append(t_hq)
        ph8[("lastt", st)] = hq_parts[-1]
        w5 = list(hq_parts)
        if st >= 1:
            w5.append(ph8[("hqSt", st - 1)])
        tok = b.dmat("sync", "hqT", hqs[:], hqb[:], waits=w5)
        ph8[("hqT", st)] = tok
        ph8[("hqSt", st)] = b.dma(
            "scalar", "hqS", hq_d[:, :, st * 128:(st + 1) * 128], hqs[:],
            waits=[tok])

    for st in range(NS):
        emit_H(st)
        if st >= 1:
            emit_quant(st - 1)
    emit_quant(NS - 1)
    f1_done = [ph8[("hmm", NS - 1, 1)], ph8[("hmm", NS - 1, 0)],
               ph8[("hqSt", NS - 1)], ph8[("hqT", NS - 1)],
               ph8[("lastt", NS - 1)], ph8[("relu", NS - 1, 0)],
               ph8[("relu", NS - 1, 1)], ph8[("qs2", NS - 1)],
               ph8[("cv", NS - 1)], ph8[("r01", NS - 1)],
               ph8[("t1", NS - 1)]]
    pop("sgnb", "c2buf", "hibuf", "dbuf", "tbuf", "rbuf1", "rbuf0",
        "hqs", "hqb", "xqc1", "xqc0")
    pop("TUB", "TUA")

    HQf = push("HQf", [128, NI, S], BF16)
    tdts = [push(f"tdts{i}", [128, NI, 128], BF16) for i in range(2)]
    gtile = [push(f"gt{i}", [128, S], BF16) for i in range(2)]
    hql = b.dma("sync", "hql", HQf[:], hq_d[:], waits=f1_done)
    for gt in range(NG):
        tb_ = tdts[gt % 2]
        w = [hql]
        if gt < 2:
            w.append(tdn_done)
        if gt >= 2:
            w.append(ph8[("fmm", gt - 2)])
        ld = b.dma("sync", f"td{gt % 2}", tb_[:], tdt_d[gt], waits=w)
        pe = pA if gt % 2 == 0 else pB
        w2 = [ld]
        if gt >= 2:
            w2.append(ph8[("gev", gt - 2)])
        else:
            w2.append(ph8[("relu", NS - 1, gt)])
        first = True
        for sb_ in range(4):
            for ci in range(NI):
                tok = b.op("tensor", lambda e, pe=pe, ci=ci, sb_=sb_, tb_=tb_:
                           e.matmul(pe[:, sb_ * 512:(sb_ + 1) * 512],
                                    tb_[:, ci, :],
                                    HQf[:, ci, sb_ * 512:(sb_ + 1) * 512],
                                    start=(ci == 0), stop=(ci == NI - 1)),
                           waits=w2 if first else ())
                first = False
        ph8[("fmm", gt)] = tok
        gq = gtile[gt % 2]
        w3 = [tok]
        if gt >= 2:
            w3.append(ph8[("gst", gt - 2)])
        ph8[("gev", gt)] = b.op("scalar", lambda e, pe=pe, gq=gq: e.activation(
            out=gq[:], in_=pe[:, :], func=AF.Silu), waits=w3)
        ph8[("gst", gt)] = b.dma(
            "scalar", f"gs{gt % 2}",
            g_d.rearrange("t p c j -> p t c j")[:, :, gt, :],
            gq[:].rearrange("p (t j) -> p t j", j=128),
            waits=[ph8[("gev", gt)]])
    f2_done = [ph8[("gst", NG - 1)], ph8[("gst", NG - 2)],
               ph8[("fmm", NG - 1)], ph8[("gev", NG - 1)],
               ph8[("gev", NG - 2)]]
    pop("gt1", "gt0", "tdts1", "tdts0", "HQf")

    if upto < 9:
        b.emit()
        return nc

    fstg = [push(f"fstg{i}", [128, NE, 128], BF16, is_transpose_dest=True)
            for i in range(2)]
    satm = [push(f"satm{i}", [128, E], BF16) for i in range(2)]
    gtm = [push(f"gtm{i}", [128, E], BF16) for i in range(2)]
    sgb = push("sgb", [128, E], BF16)
    xinf = push("xinf", [128, E], F32)
    outt = [push(f"outt{i}", [128, E], F32) for i in range(2)]
    ph9 = {}
    for st in range(NS):
        fs = fstg[st % 2]
        sm = satm[st % 2]
        gm = gtm[st % 2]
        w = list(satall) + (list(f2_done) if st < 2 else [])
        if st >= 2:
            w.append(ph9[("sg", st - 2)])
        sld = b.dma("sync", f"f9{st % 2}", fs[:], sat_d[st], waits=w)
        sT = b.dmat("sync", f"f9{st % 2}",
                    sm[:].rearrange("p (c j) -> p c j", j=128),
                    fs[:].rearrange("p c j -> p (c j)"), waits=[sld])
        gld = b.dma("sync", f"f9{st % 2}", fs[:], g_d[st],
                    waits=[ph8[("gst", NG - 1)], ph8[("gst", NG - 2)], sT])
        gT = b.dmat("sync", f"f9{st % 2}",
                    gm[:].rearrange("p (c j) -> p c j", j=128),
                    fs[:].rearrange("p c j -> p (c j)"), waits=[gld])
        w2 = (list(f2_done) if st == 0 else []) + \
            ([ph9[("add", st - 1)]] if st >= 1 else [])
        xl = b.dma("scalar", "xr", xinf[:], x_t[st], waits=w2)
        w3 = [sT, gT]
        if st >= 1:
            w3.append(ph9[("add", st - 1)])
        ph9[("sg", st)] = b.op("vector", lambda e, sm=sm, gm=gm: e.tensor_mul(
            out=sgb[:], in0=sm[:], in1=gm[:]), waits=w3)
        ob = outt[st % 2]
        w4 = [ph9[("sg", st)], xl]
        if st >= 2:
            w4.append(ph9[("ost", st - 2)])
        ph9[("add", st)] = b.op("vector", lambda e, ob=ob: e.tensor_add(
            out=ob[:], in0=xinf[:], in1=sgb[:]), waits=w4)
        ph9[("ost", st)] = b.dma(
            "sync", f"oo{st % 2}",
            out_d.rearrange("(t p) e -> t p e", p=128)[st], ob[:],
            waits=[ph9[("add", st)]])

    b.waits("sync", [ph9[("ost", NS - 1)], ph9[("ost", NS - 2)]])
    b.emit()
    return nc


def kernel(L0_attn_x, L0_attn_Wq, L0_attn_Wk, L0_attn_Wv, L0_attn_Wo,
           L0_ffn_w_up, L0_ffn_w_down):
    xs = np.ascontiguousarray(np.asarray(L0_attn_x, dtype=np.float32))
    ws = {
        "wq": np.ascontiguousarray(np.asarray(L0_attn_Wq, dtype=np.float32)),
        "wk": np.ascontiguousarray(np.asarray(L0_attn_Wk, dtype=np.float32)),
        "wv": np.ascontiguousarray(np.asarray(L0_attn_Wv, dtype=np.float32)),
        "wo": np.ascontiguousarray(np.asarray(L0_attn_Wo, dtype=np.float32)),
        "wup": np.ascontiguousarray(np.asarray(L0_ffn_w_up, dtype=np.float32)),
        "wdn": np.ascontiguousarray(np.asarray(L0_ffn_w_down, dtype=np.float32)),
    }
    nc = build_program()
    in_maps = [dict(ws, x=xs[i]) for i in range(8)]
    res = run_bass_kernel_spmd(nc, in_maps, core_ids=list(range(8)))
    return np.stack([np.asarray(r["out"]) for r in res.results], axis=0)


# version 09
# speedup vs baseline: 1.2721x; 1.2542x over previous
import numpy as np
import concourse.bass as bass
import concourse.mybir as mybir
from concourse.bass_utils import run_bass_kernel_spmd

F32 = mybir.dt.float32
BF16 = mybir.dt.bfloat16
AX = mybir.AxisListType.X
ALU = mybir.AluOpType
AF = mybir.ActivationFunctionType

S = 2048
E = 2048
I = 4096
NS = S // 128
NE = E // 128
NI = I // 128
NG = E // 128
EPS = 1e-5


class Ctr:
    def __init__(self, nc, name):
        self.guard = nc.semaphore(name)
        self.h = self.guard.__enter__()
        self.n = 0


class B:
    ENGINES = ("sync", "scalar", "vector", "tensor", "gpsimd")

    def __init__(self, nc):
        self.nc = nc
        self.streams = {e: [] for e in self.ENGINES}
        self.sems = {}

    def sem(self, name):
        if name not in self.sems:
            self.sems[name] = Ctr(self.nc, name)
        return self.sems[name]

    def waits(self, eng, toks):
        best = {}
        for t in toks:
            if t is None:
                continue
            s, v = t
            if s not in best or best[s] < v:
                best[s] = v
        for s, v in best.items():
            self.streams[eng].append(lambda e, s=s, v=v: e.wait_ge(s.h, v))

    def op(self, eng, fn, waits=(), sem=None):
        self.waits(eng, waits)
        s = self.sem(sem or ("c_" + eng))
        s.n += 1
        v = s.n
        self.streams[eng].append(lambda e, fn=fn, s=s: fn(e).then_inc(s.h, 1))
        return (s, v)

    def dma(self, eng, ch, out, in_, waits=(), **kw):
        self.waits(eng, waits)
        s = self.sem("d_" + ch)
        s.n += 16
        v = s.n
        self.streams[eng].append(
            lambda e, out=out, in_=in_, s=s, kw=kw:
            e.dma_start(out=out, in_=in_, **kw).then_inc(s.h, 16))
        return (s, v)

    def dmat(self, eng, ch, out, in_, waits=()):
        self.waits(eng, waits)
        s = self.sem("d_" + ch)
        s.n += 16
        v = s.n
        self.streams[eng].append(
            lambda e, out=out, in_=in_, s=s:
            e.dma_start_transpose(out=out, in_=in_).then_inc(s.h, 16))
        return (s, v)

    def emit(self):
        with self.nc.Block() as blk:
            for name in self.ENGINES:
                items = self.streams[name]
                if not items:
                    continue
                getattr(blk, name)(lambda e, items=items: [f(e) for f in items])


def build_program(upto=9, debug=()):
    nc = bass.Bass("TRN2", target_bir_lowering=False, debug=False)
    b = B(nc)

    x_d = nc.dram_tensor("x", [S, E], F32, kind="ExternalInput").ap()
    wq_d = nc.dram_tensor("wq", [E, E], F32, kind="ExternalInput").ap()
    wk_d = nc.dram_tensor("wk", [E, E], F32, kind="ExternalInput").ap()
    wv_d = nc.dram_tensor("wv", [E, E], F32, kind="ExternalInput").ap()
    wo_d = nc.dram_tensor("wo", [E, E], F32, kind="ExternalInput").ap()
    wup_d = nc.dram_tensor("wup", [I, E], F32, kind="ExternalInput").ap()
    wdn_d = nc.dram_tensor("wdn", [E, I], F32, kind="ExternalInput").ap()
    out_d = nc.dram_tensor("out", [S, E], F32, kind="ExternalOutput").ap()

    def scratch(name, shape, dtype=BF16):
        kind = "ExternalOutput" if name in debug else "Internal"
        return nc.dram_tensor(name, shape, dtype, kind=kind).ap()

    xn_d = scratch("xn_s", [128, NE, S])
    xq_d = scratch("xq_s", [NS, 128, NE, 128])
    q_d = scratch("q_s", [NE, 128, NS, 128])
    ot_d = scratch("ot_s", [NE, 128, S])
    sat_d = scratch("sat_s", [NS, 128, NG, 128])
    g_d = scratch("g_s", [NS, 128, NG, 128])
    tdt_d = scratch("tdt_s", [NG, 128, NI, 128])
    hq_d = scratch("hq_s", [128, NI, S])

    live = {}

    def push(name, shape, dtype, **kw):
        g = nc.sbuf_tensor(name, shape, dtype, **kw)
        h = g.__enter__()
        live[name] = g
        return h

    def pop(*names):
        for name in names:
            live.pop(name).__exit__(None, None, None)

    def ppush(name, shape):
        g = nc.psum_tensor(name, shape, F32)
        h = g.__enter__()
        live[name] = g
        return h

    st6 = push("st6", [128, 24], F32)
    mv = push("mv", [128, 2], F32)
    sdv = push("sdv", [128, 1], F32)
    rstd = push("rstd", [128, 1], F32)
    amax = push("amax", [128, 1], F32)
    qs_all = push("qs_all", [128, NS], F32)
    qr_all = push("qr_all", [128, NS], F32)
    mrow = push("mrow", [128, 1], F32)
    nmrow = push("nmrow", [128, 1], F32)
    zsum = push("zsum", [128, 1], F32)
    rz_all = push("rz_all", [128, NE], F32)
    wacc_up = push("wacc_up", [128, 1], F32)
    wacc_dn = push("wacc_dn", [128, 1], F32)
    wred = push("wred", [128, 1], F32)
    wred2 = push("wred2", [128, 1], F32)
    ones_f = push("ones_f", [1, 128], F32)
    onescol = push("onescol", [128, 1], F32)
    srow = push("srow", [1, 8], F32)
    sup1 = push("sup1", [1, 1], F32)
    sdn1 = push("sdn1", [1, 1], F32)
    bcast = push("bcast", [128, 8], F32)
    rmax = push("rmax", [128, 2], F32)
    rmx = push("rmx", [128, 1], F32)
    maxh = push("maxh", [128, 1], F32)
    rcph = push("rcph", [128, 1], F32)
    t1 = push("t1", [128, 1], F32)
    u1 = push("u1", [128, 1], F32)
    qs2 = push("qs2", [128, 1], F32)
    sqsb = push("sqsb", [128, 1], F32)
    cvec = push("cvec", [128, 1], F32)
    cpool = push("cpool", [128, 2], F32)

    pA = ppush("pA", [128, 2048])
    pB = ppush("pB", [128, 2048])

    T = {}

    tk = b.op("gpsimd", lambda e: e.memset(cpool[:, 0:1], EPS))
    tk = b.op("gpsimd", lambda e: e.memset(cpool[:, 1:2], 256.5), waits=[tk])
    nc.const_aps.register(F32, EPS, cpool[:, 0:1])
    nc.const_aps.register(F32, 256.5, cpool[:, 1:2])
    wz_up = b.op("vector", lambda e: e.memset(wacc_up[:], 0.0))
    wz_dn = b.op("vector", lambda e: e.memset(wacc_dn[:], 0.0), waits=[wz_up])
    tk = b.op("gpsimd", lambda e: e.memset(ones_f[:], 1.0), waits=[tk])
    tk = b.op("gpsimd", lambda e: e.memset(onescol[:], 1.0), waits=[tk])
    gp_init = tk

    wup_t = wup_d.rearrange("(t p) e -> t p e", p=128)
    wdn_t = wdn_d.rearrange("(t p) i -> t p i", p=128)
    wstat = [None, None]
    ws_state = {"i": 0, "buf_tok": [None, None], "acc_up": None, "acc_dn": None}

    def stream_wstats(n):
        st_ = ws_state
        for _ in range(n):
            i = st_["i"]
            if i >= 64:
                return
            buf = wstat[i % 2]
            if i < 32:
                src = wup_t[i][:, :]
                acc, red = wacc_up, wred
            else:
                j = i - 32
                src = wdn_t[j // 2][:, (j % 2) * 2048:(j % 2 + 1) * 2048]
                acc, red = wacc_dn, wred2
            ld = b.dma("gpsimd", f"ws{i % 2}", buf[:], src,
                       waits=[st_["buf_tok"][i % 2]])
            key = "acc_up" if i < 32 else "acc_dn"
            init = st_[key] if st_[key] else (wz_up if i < 32 else wz_dn)
            tok = b.op("vector", lambda e, buf=buf, red=red: e.reduce_sum(
                out=red[:], in_=buf[:], axis=AX, apply_absolute_value=True),
                waits=[ld, init])
            st_[key] = b.op("vector", lambda e, acc=acc, red=red: e.tensor_add(
                out=acc[:], in0=acc[:], in1=red[:]), waits=[tok, init])
            st_["buf_tok"][i % 2] = st_[key]
            st_["i"] += 1

    wstat[0] = push("wst0", [128, 2048], F32, side="right")
    wstat[1] = push("wst1", [128, 2048], F32, side="right")
    xin = [push(f"xin{i}", [128, E], F32) for i in range(2)]
    xnf = push("xnf32", [128, E], F32)
    xnb = [push(f"xnb{i}", [128, E], BF16) for i in range(2)]
    xqb = push("xqb", [128, E], BF16)
    xnt = push("xnt", [128, NE, 128], BF16, is_transpose_dest=True)
    xqt = push("xqt", [128, NE, 128], BF16, is_transpose_dest=True)
    ybc = push("ybc", [128, 1024], F32)
    tb = push("tb", [128, 1024], F32)
    ybi = push("ybi", [128, 1024], mybir.dt.int32)
    ccb = push("ccb", [128, 1024], F32)
    ccb2 = push("ccb2", [128, 1024], F32)

    wfQ = push("wfQ", [128, NE, E], BF16)
    xbQ = [push(f"xbQ{i}", [128, NE, 512], BF16) for i in range(2)]
    ebQ = [push(f"ebQ{i}", [128, E], BF16) for i in range(2)]
    qtoks = {}
    wlQ = []
    for cg in range(4):
        wlQ.append(b.dma("gpsimd", f"wf{cg}",
                         wfQ[:, cg * 4:(cg + 1) * 4, :],
                         wq_d[cg * 512:(cg + 1) * 512, :]
                         .rearrange("(c p) f -> p c f", p=128), waits=[]))

    def q_sg(sg):
        xw = [T[("xnS", sg * 4 + 3)]]
        if sg >= 2:
            xw.append(qtoks[("mm", sg - 2, 3)])
        xl = b.dma("sync", f"xb{sg % 2}", xbQ[sg % 2][:],
                   xn_d[:, :, sg * 512:(sg + 1) * 512], waits=xw)
        for stl in range(4):
            st = sg * 4 + stl
            pe = pA if st % 2 == 0 else pB
            first_w = [xl, wlQ[0]]
            if st >= 2:
                first_w.append(qtoks[("ev", st - 2)])
            for ce in range(NE):
                w2 = first_w if ce == 0 else \
                    ([wlQ[ce // 4]] if ce % 4 == 0 else ())
                for fb in range(4):
                    tok = b.op("tensor",
                               lambda e, pe=pe, ce=ce, fb=fb, stl=stl,
                               xb=xbQ[sg % 2]: e.matmul(
                                   pe[:, fb * 512:(fb + 1) * 512],
                                   xb[:, ce, stl * 128:(stl + 1) * 128],
                                   wfQ[:, ce, fb * 512:(fb + 1) * 512],
                                   start=(ce == 0), stop=(ce == NE - 1)),
                               waits=w2 if fb == 0 else ())
                    w2 = ()
            qtoks[("mm", sg, stl)] = tok
            eb = ebQ[st % 2]
            w3 = [tok]
            if st >= 2:
                w3.append(qtoks[("stq", st - 2)])
            ev = b.op("scalar", lambda e, pe=pe, eb=eb: e.activation(
                out=eb[:], in_=pe[:, :], func=AF.Copy), waits=w3)
            qtoks[("ev", st)] = ev
            qtoks[("stq", st)] = b.dma(
                "scalar", f"qs{st % 2}",
                q_d.rearrange("t p c j -> p t c j")[:, :, st, :],
                eb[:].rearrange("p (t j) -> p t j", j=128), waits=[ev])

    x_t = x_d.rearrange("(t p) e -> t p e", p=128)
    for st in range(NS):
        xb = xin[st % 2]
        w = []
        if st >= 2:
            w = [T[("xn", st - 2)], T[("bn", st - 2)]]
        T[("xload", st)] = b.dma("sync", f"x{st % 2}", xb[:], x_t[st], waits=w)
        bns = []
        for c in range(4):
            w = [T[("xload", st)]]
            if c == 0 and st >= 1:
                w.append(T[("bn", st - 1)])
            bns.append(b.op("vector", lambda e, xb=xb, c=c: e.bn_stats(
                out=st6[:, c * 6:(c + 1) * 6], in_=xb[:, c * 512:(c + 1) * 512]),
                waits=w))
        w = list(bns)
        if st >= 1:
            w += [T[("sq", st - 1)], T[("xn", st - 1)]]
        T[("bn", st)] = b.op("vector", lambda e: e.bn_aggr(
            out=mv[:], in_=st6[:]), waits=w)
        T[("sq", st)] = b.op("scalar", lambda e: e.activation(
            out=sdv[:], in_=mv[:, 1:2], func=AF.Sqrt, bias=EPS),
            waits=[T[("bn", st)]] + ([gp_init] if st == 0 else []))
        w = [T[("sq", st)]]
        if st >= 1:
            w += [T[("xn", st - 1)], T[("qs", st - 1)]]
        T[("rstd", st)] = b.op("vector", lambda e: e.reciprocal(
            out=rstd[:], in_=sdv[:]), waits=w)
        xnt_b = xnb[st % 2]
        w = [T[("rstd", st)]]
        if st >= 1:
            w += [T[("absmax", st - 1)], T[("y", st - 1, 1)],
                  T[("xnb", st - 1)]]
        T[("xn", st)] = b.op("vector", lambda e, xb=xb:
                             e.tensor_scalar(
                                 out=xnf[:], in0=xb[:], scalar1=mv[:, 0:1],
                                 scalar2=rstd[:], op0=ALU.subtract,
                                 op1=ALU.mult), waits=w)
        w = [T[("xn", st)]]
        if st >= 2:
            w.append(T[("xnT", st - 2)])
        T[("xnb", st)] = b.op("vector", lambda e, xnt_b=xnt_b: e.tensor_copy(
            out=xnt_b[:], in_=xnf[:]), waits=w)
        T[("absmax", st)] = b.op("vector", lambda e: e.reduce_max(
            out=amax[:], in_=xnf[:], axis=AX, apply_absolute_value=True),
            waits=[T[("xn", st)]])
        tok = b.op("vector", lambda e: e.tensor_scalar_max(
            out=amax[:], in0=amax[:], scalar1=EPS), waits=[T[("absmax", st)]])
        tok = b.op("vector", lambda e, st=st: e.tensor_scalar_mul(
            out=qr_all[:, st:st + 1], in0=amax[:], scalar1=1.0 / 127.0),
            waits=[tok])
        tok2 = b.op("vector", lambda e: e.reciprocal(out=rstd[:], in_=amax[:]),
                    waits=[tok, T[("xn", st)]])
        T[("qs", st)] = b.op("vector", lambda e, st=st: e.tensor_scalar_mul(
            out=qs_all[:, st:st + 1], in0=rstd[:], scalar1=127.0), waits=[tok2])
        prev = None
        for ch in range(2):
            sl = slice(ch * 1024, (ch + 1) * 1024)
            w = [T[("qs", st)]]
            if prev is not None:
                w.append(prev)
            elif st >= 1:
                w.append(T[("xq", st - 1, 1)])
            ty = b.op("scalar",
                lambda e, st=st, sl=sl: e.activation(
                    out=ybc[:], in_=xnf[:, sl], func=AF.Copy,
                    scale=qs_all[:, st:st + 1]), waits=w)
            T[("y", st, ch)] = ty
            tyi = b.op("vector", lambda e: e.tensor_copy(
                out=ybi[:], in_=ybc[:]), waits=[ty])
            td = b.op("vector", lambda e: e.tensor_sub(
                out=tb[:], in0=ybc[:], in1=ybi[:]), waits=[tyi])
            tca = b.op("vector", lambda e: e.tensor_single_scalar(
                out=ccb[:], in_=tb[:], scalar=0.5, op=ALU.is_gt), waits=[td])
            tcb = b.op("vector", lambda e: e.scalar_tensor_tensor(
                out=ybc[:], in0=tb[:], scalar=1.5, in1=ccb[:],
                op0=ALU.is_gt, op1=ALU.add), waits=[tca])
            tcc = b.op("vector", lambda e: e.tensor_scalar(
                out=ccb[:], in0=tb[:], scalar1=-0.5, scalar2=-1.0,
                op0=ALU.is_lt, op1=ALU.mult), waits=[tcb])
            tcd = b.op("vector", lambda e: e.scalar_tensor_tensor(
                out=ccb2[:], in0=tb[:], scalar=-1.5, in1=ccb[:],
                op0=ALU.is_lt, op1=ALU.subtract), waits=[tcc])
            tce = b.op("vector", lambda e: e.tensor_sub(
                out=tb[:], in0=ybc[:], in1=ccb2[:]), waits=[tcd])
            wq_ = [tce]
            if ch == 0 and st >= 1:
                wq_.append(T[("xqT", st - 1)])
            T[("xq", st, ch)] = b.op("vector",
                lambda e, sl=sl: e.tensor_add(
                    out=xqb[:, sl], in0=ybi[:], in1=tb[:]), waits=wq_)
            prev = T[("xq", st, ch)]
        w = [T[("xnb", st)]]
        if st >= 1:
            w.append(T[("xnS", st - 1)])
        T[("xnT", st)] = b.dmat("sync", "xnT", xnt[:], xnt_b[:], waits=w)
        T[("xnS", st)] = b.dma("scalar", "xnS",
                               xn_d[:, :, st * 128:(st + 1) * 128], xnt[:],
                               waits=[T[("xnT", st)]])
        w = [T[("xq", st, 1)]]
        if st >= 1:
            w.append(T[("xqS", st - 1)])
        T[("xqT", st)] = b.dmat("sync", "xqT", xqt[:], xqb[:], waits=w)
        T[("xqS", st)] = b.dma("scalar", "xqS", xq_d[st], xqt[:],
                               waits=[T[("xqT", st)]])
        stream_wstats(1)
        if st % 4 == 3:
            q_sg(st // 4)

    xn_done = T[("xnS", NS - 1)]
    ph1_done = [T[("bn", NS - 1)], T[("absmax", NS - 1)],
                T[("xq", NS - 1, 1)], T[("xnS", NS - 1)], T[("xqS", NS - 1)],
                T[("sq", NS - 1)], T[("qs", NS - 1)], T[("xn", NS - 1)],
                T[("xnb", NS - 1)], T[("y", NS - 1, 1)]]
    if upto < 2:
        b.emit()
        return nc

    def proj_tm(w_dram, phase, dst_sb, psum_last, prev_done):
        wf = push(f"wf{phase}", [128, NE, E], BF16)
        xtl = [push(f"xt{phase}{i}", [128, NE, 128], BF16) for i in range(2)]
        toks = {}
        wl = []
        for cg in range(4):
            wl.append(b.dma("gpsimd", f"wf{cg}",
                            wf[:, cg * 4:(cg + 1) * 4, :],
                            w_dram[cg * 512:(cg + 1) * 512, :]
                            .rearrange("(c p) f -> p c f", p=128),
                            waits=list(prev_done)))
        for st in range(NS):
            xw = [T[("xnS", st)]] + (list(prev_done) if st < 2 else [])
            if st >= 2:
                xw.append(toks[("mm", st - 2)])
            xl = b.dma("sync", f"xt{st % 2}", xtl[st % 2][:],
                       xn_d[:, :, st * 128:(st + 1) * 128], waits=xw)
            pe = pA if st % 2 == 0 else pB
            first_w = [xl, wl[0]]
            if st >= 2:
                first_w.append(toks[("ev", st - 2)])
            elif psum_last[st % 2] is not None:
                first_w.append(psum_last[st % 2])
            for ce in range(NE):
                w2 = first_w if ce == 0 else \
                    ([wl[ce // 4]] if ce % 4 == 0 else ())
                for fb in range(4):
                    tok = b.op("tensor",
                               lambda e, pe=pe, ce=ce, fb=fb,
                               xb=xtl[st % 2]: e.matmul(
                                   pe[:, fb * 512:(fb + 1) * 512],
                                   xb[:, ce, :],
                                   wf[:, ce, fb * 512:(fb + 1) * 512],
                                   start=(ce == 0), stop=(ce == NE - 1)),
                               waits=w2 if fb == 0 else ())
                    w2 = ()
            toks[("mm", st)] = tok
            ev = b.op("scalar", lambda e, pe=pe, st=st: e.activation(
                out=dst_sb[:, st, :], in_=pe[:, :], func=AF.Copy),
                waits=[tok])
            toks[("ev", st)] = ev
            stream_wstats(4)
        res_last = {0: toks[("ev", NS - 2)], 1: toks[("ev", NS - 1)]}
        mm_last = toks[("mm", NS - 1)]
        res_done = toks[("ev", NS - 1)]
        pop(f"xt{phase}1", f"xt{phase}0", f"wf{phase}")
        return res_done, res_last, mm_last, []

    q_done = qtoks[("stq", NS - 1)]
    q_extra = [qtoks[("stq", NS - 2)]]
    q_mm_last = qtoks[("mm", 3, 3)]
    plast = {0: qtoks[("ev", NS - 2)], 1: qtoks[("ev", NS - 1)]}
    pop("ebQ1", "ebQ0", "xbQ1", "xbQ0", "wfQ")
    pop("ccb2", "ccb", "ybi", "tb", "ybc", "xqt", "xnt", "xqb", "xnb1",
        "xnb0", "xnf32", "xin1", "xin0")
    mskc = push("mskc", [128, 2048], BF16)
    ttile = [push(f"tt{i}", [128, 2048], BF16) for i in range(2)]
    tstg = push("tstg", [128, 16, 128], BF16, is_transpose_dest=True)
    wtern = [push("wst0b", [128, 2048], F32),
             push("wst1b", [128, 2048], F32)]
    K_sb = push("K_sb", [128, NS, E], BF16)
    k_done, plast, k_mm_last, _ = proj_tm(
        wk_d, "K", K_sb, plast, [q_mm_last, q_done] + q_extra + ph1_done)
    stream_wstats(64)
    ws_done = [ws_state["acc_up"], ws_state["acc_dn"]]
    pop("wst1", "wst0")

    mm1 = b.op("tensor", lambda e: e.matmul(
        pB[0:1, 1536:1537], onescol[:], wacc_up[:], start=True, stop=True),
        waits=ws_done + [gp_init, plast[1], k_mm_last])
    mm2 = b.op("tensor", lambda e: e.matmul(
        pB[0:1, 1544:1545], onescol[:], wacc_dn[:], start=True, stop=True),
        waits=[])
    t_a = b.op("vector", lambda e: e.tensor_scalar(
        out=sup1[:], in0=pB[0:1, 1536:1537], scalar1=1.0 / (I * E),
        scalar2=EPS, op0=ALU.mult, op1=ALU.max), waits=[mm1])
    t_b = b.op("vector", lambda e: e.tensor_scalar(
        out=sdn1[:], in0=pB[0:1, 1544:1545], scalar1=1.0 / (I * E),
        scalar2=EPS, op0=ALU.mult, op1=ALU.max), waits=[mm2])
    packs = [
        b.op("vector", lambda e: e.tensor_scalar_mul(
            out=srow[:, 0:1], in0=sup1[:], scalar1=0.5), waits=[t_a]),
        b.op("vector", lambda e: e.tensor_scalar_mul(
            out=srow[:, 1:2], in0=sup1[:], scalar1=-0.5), waits=[t_a]),
        b.op("vector", lambda e: e.tensor_scalar_mul(
            out=srow[:, 2:3], in0=sdn1[:], scalar1=0.5), waits=[t_b]),
        b.op("vector", lambda e: e.tensor_scalar_mul(
            out=srow[:, 3:4], in0=sdn1[:], scalar1=-0.5), waits=[t_b]),
        b.op("vector", lambda e: e.tensor_scalar_mul(
            out=srow[:, 4:5], in0=sup1[:], scalar1=1.0), waits=[t_a]),
        b.op("vector", lambda e: e.tensor_scalar_mul(
            out=srow[:, 5:6], in0=sdn1[:], scalar1=1.0), waits=[t_b]),
        b.op("vector", lambda e: e.memset(srow[:, 6:8], 0.0)),
    ]
    mm3 = b.op("tensor", lambda e: e.matmul(
        pB[:, 1552:1560], ones_f[:], srow[:], start=True, stop=True),
        waits=packs)
    bctok = b.op("vector", lambda e: e.tensor_copy(
        out=bcast[:], in_=pB[:, 1552:1560]), waits=[mm3])

    if upto < 4:
        b.emit()
        return nc

    V_sb = push("V_sb", [128, NE, S], BF16)
    XNh = push("XNh", [128, NE, 1024], BF16)
    wcol = [push(f"wcol{i}", [128, NE, 128], BF16) for i in range(2)]
    ph7 = {}

    def down_unit(k):
        gc, hh = k // 2, k % 2
        buf = wtern[k % 2]
        w = [bctok] if k < 2 else [ph7[("dtern", k - 2)]]
        ld = b.dma("sync", f"tw{k % 2}", buf[:],
                   wdn_t[gc][:, hh * 2048:(hh + 1) * 2048], waits=w)
        w = [ld]
        if k >= 1:
            w.append(ph7[("dtern", k - 1)])
        msk = b.op("vector", lambda e, buf=buf: e.tensor_single_scalar(
            out=mskc[:], in_=buf[:], scalar=bcast[:, 3:4], op=ALU.is_le),
            waits=w)
        tt = ttile[k % 2]
        w2 = [msk]
        if k >= 2:
            w2.append(ph7[("dT", k - 2)])
        ph7[("dtern", k)] = b.op("vector", lambda e, buf=buf, tt=tt:
            e.scalar_tensor_tensor(
                out=tt[:], in0=buf[:], scalar=bcast[:, 2:3],
                in1=mskc[:], op0=ALU.is_ge, op1=ALU.subtract), waits=w2)
        w3 = [ph7[("dtern", k)]]
        if k >= 1:
            w3.append(ph7[("dst", k - 1)])
        tT = b.dmat("sync", f"dT{k % 2}", tstg[:], tt[:], waits=w3)
        ph7[("dT", k)] = tT
        ph7[("dst", k)] = b.dma(
            "scalar", f"dst{k % 2}", tdt_d[gc][:, hh * 16:(hh + 1) * 16, :],
            tstg[:], waits=[tT])

    ph4_prev = [k_mm_last, k_done, q_done] + q_extra + ws_done
    VSET = [pA[:, 0:1024], pA[:, 1024:2048], pB[:, 0:1024]]
    vset_last = {0: plast[0], 1: plast[0], 2: plast[1]}
    vtoks = {}
    xnl = [None, None]
    vit = 0
    for h in range(2):
        w = [xn_done] + (ph4_prev if h == 0 else [vtoks[("mm", 0, NE - 1)]])
        xnl[h] = b.dma("sync", "xnf", XNh[:],
                       xn_d[:, :, h * 1024:(h + 1) * 1024], waits=w)
        for ft in range(NE):
            buf = wcol[ft % 2]
            w = [gp_init] + list(ph4_prev) if (h == 0 and ft < 2) else \
                [vtoks[("mm", h, ft - 2)] if ft >= 2 else
                 vtoks[("mm", 0, NE - 2 + ft)]]
            ldw = b.dma("gpsimd", f"wc{ft % 2}", buf[:],
                        wv_d[:, ft * 128:(ft + 1) * 128]
                        .rearrange("(c p) f -> p c f", p=128), waits=w)
            pe = VSET[vit % 3]
            w2 = [ldw, xnl[h], vset_last[vit % 3]]
            first = True
            for sbl in range(2):
                for ce in range(NE):
                    tok = b.op("tensor",
                               lambda e, pe=pe, ce=ce, sbl=sbl, buf=buf:
                               e.matmul(pe[:, sbl * 512:(sbl + 1) * 512],
                                        buf[:, ce, :],
                                        XNh[:, ce, sbl * 512:(sbl + 1) * 512],
                                        start=(ce == 0), stop=(ce == NE - 1)),
                               waits=w2 if first else ())
                    first = False
            vtoks[("mm", h, ft)] = tok
            ev = b.op("scalar", lambda e, pe=pe, ft=ft, h=h: e.activation(
                out=V_sb[:, ft, h * 1024:(h + 1) * 1024], in_=pe,
                func=AF.Copy), waits=[tok])
            vtoks[("ev", h, ft)] = ev
            vset_last[vit % 3] = ev
            vit += 1
            down_unit(h * NE + ft)
    v_done = vtoks[("ev", 1, NE - 1)]
    v_mm_last = vtoks[("mm", 1, NE - 1)]
    tdn_done = ph7[("dst", 2 * NE - 1)]
    dn_tail = [tdn_done, ph7[("dst", 2 * NE - 2)], ph7[("dT", 2 * NE - 1)],
               ph7[("dtern", 2 * NE - 1)], ph7[("dtern", 2 * NE - 2)]]
    pop("wcol1", "wcol0", "XNh")

    if upto < 5:
        b.emit()
        return nc

    qcol = [push(f"qcol{i}", [128, NS, 128], BF16) for i in range(2)]
    ztile = [push(f"zt{i}", [128, E], BF16) for i in range(2)]
    ztp = [push(f"ztp{i}", [128, NE, 128], BF16, is_transpose_dest=True)
           for i in range(2)]
    otile = [push(f"oti{i}", [128, E], BF16) for i in range(2)]
    mrow2 = push("mrow2", [128, 1], F32)
    zsa = push("zsa", [128, 1], F32)
    zsb = push("zsb", [128, 1], F32)
    ASET = [pA[:, 0:1024], pA[:, 1024:2048], pB[:, 0:1024]]
    ph5 = {}

    def emit_O(et):
        zp = ztp[et % 2]
        ot = otile[et % 2]
        for sb_ in range(4):
            pe = pB[:, 1024 + (sb_ % 2) * 512:1536 + (sb_ % 2) * 512]
            w2 = [ph5[("zT", et)]]
            if et == 0 and sb_ < 2:
                w2.append(bctok)
            if sb_ >= 2:
                w2.append(ph5[("oev", et, sb_ - 2)])
            elif et >= 1:
                w2.append(ph5[("oev", et - 1, sb_ + 2)])
            for cf in range(NE):
                tok = b.op("tensor", lambda e, pe=pe, cf=cf, sb_=sb_, zp=zp:
                           e.matmul(pe, zp[:, cf, :],
                                    V_sb[:, cf, sb_ * 512:(sb_ + 1) * 512],
                                    start=(cf == 0), stop=(cf == NE - 1)),
                           waits=w2 if cf == 0 else ())
            ph5[("Omm", et, sb_)] = tok
            w3 = [tok, ph5[("rz", et)]]
            if et >= 2:
                w3.append(ph5[("ost", et - 2)])
            ph5[("oev", et, sb_)] = b.op(
                "scalar", lambda e, pe=pe, ot=ot, sb_=sb_, et=et: e.activation(
                    out=ot[:, sb_ * 512:(sb_ + 1) * 512], in_=pe, func=AF.Copy,
                    scale=rz_all[:, et:et + 1]), waits=w3)
        ph5[("ost", et)] = b.dma("scalar", f"os{et % 2}", ot_d[et],
                                 otile[et % 2][:],
                                 waits=[ph5[("oev", et, s_)] for s_ in range(4)])

    for et in range(NE):
        qc = qcol[et % 2]
        w = [q_done]
        if et < 2:
            w.append(v_mm_last)
        if et >= 2:
            w.append(ph5[("Amm", et - 2, 1)])
        ph5[("qld", et)] = b.dma("sync", f"qc{et % 2}", qc[:], q_d[et],
                                 waits=w)
        for h in range(2):
            k = 2 * et + h
            pe = ASET[k % 3]
            w = [ph5[("qld", et)]]
            if k < 3:
                w.append(vset_last[k % 3])
            if et == 0 and h == 0:
                w += [k_done, v_done]
            if k >= 3:
                kp = k - 3
                w.append(ph5[("exph", kp // 2, kp % 2)])
            first = True
            for cs in range(NS):
                for fb in (0, 1):
                    f0 = (2 * h + fb) * 512
                    tok = b.op("tensor",
                               lambda e, pe=pe, cs=cs, f0=f0, qc=qc: e.matmul(
                                   pe[:, (f0 % 1024):(f0 % 1024) + 512],
                                   qc[:, cs, :],
                                   K_sb[:, cs, f0:f0 + 512],
                                   start=(cs == 0), stop=(cs == NS - 1)),
                               waits=w if first else ())
                    first = False
            ph5[("Amm", et, h)] = tok
            mxb = mrow if h == 0 else mrow2
            w = [tok]
            if et >= 1:
                w.append(ph5[("nm", et - 1)])
            ph5[("mxh", et, h)] = b.op("vector",
                lambda e, pe=pe, mxb=mxb: e.reduce_max(
                    out=mxb[:], in_=pe, axis=AX), waits=w)
        mxc = b.op("vector", lambda e: e.tensor_max(
            out=mrow[:], in0=mrow[:], in1=mrow2[:]),
            waits=[ph5[("mxh", et, 0)], ph5[("mxh", et, 1)]])
        w = [mxc]
        if et >= 1:
            w.append(ph5[("exph", et - 1, 1)])
        ph5[("nm", et)] = b.op("scalar", lambda e: e.activation(
            out=nmrow[:], in_=mrow[:], func=AF.Copy, scale=-0.125), waits=w)
        zt = ztile[et % 2]
        for h in range(2):
            pe = ASET[(2 * et + h) % 3]
            zs = zsa if h == 0 else zsb
            w = [ph5[("nm", et)]]
            if et >= 2 and h == 0:
                w.append(ph5[("zT", et - 2)])
            if et >= 1:
                w.append(ph5[("rz", et - 1)])
            ph5[("exph", et, h)] = b.op("scalar",
                lambda e, pe=pe, zt=zt, h=h, zs=zs: e.activation(
                    out=zt[:, h * 1024:(h + 1) * 1024], in_=pe, func=AF.Exp,
                    bias=nmrow[:], scale=0.125, accum_out=zs[:]), waits=w)
        zadd = b.op("vector", lambda e: e.tensor_add(
            out=zsa[:], in0=zsa[:], in1=zsb[:]),
            waits=[ph5[("exph", et, 0)], ph5[("exph", et, 1)]])
        ph5[("rz", et)] = b.op("vector", lambda e, et=et: e.reciprocal(
            out=rz_all[:, et:et + 1], in_=zsa[:]), waits=[zadd])
        w = [ph5[("exph", et, 0)], ph5[("exph", et, 1)]]
        if et >= 2:
            w.append(ph5[("Omm", et - 2, 3)])
        ph5[("zT", et)] = b.dmat("sync", f"zt{et % 2}", ztp[et % 2][:], zt[:],
                                 waits=w)
        if et >= 1:
            emit_O(et - 1)
    emit_O(NE - 1)
    ph5_done = [ph5[("Omm", NE - 1, 3)], ph5[("oev", NE - 1, 3)],
                ph5[("oev", NE - 1, 2)], ph5[("exph", NE - 1, 0)],
                ph5[("exph", NE - 1, 1)], ph5[("exph", NE - 2, 1)],
                ph5[("rz", NE - 1)], ph5[("Amm", NE - 1, 1)],
                ph5[("zT", NE - 1)], ph5[("zT", NE - 2)],
                ph5[("ost", NE - 1)], ph5[("ost", NE - 2)],
                ph5[("nm", NE - 1)],
                ph5[("oev", NE - 1, 0)], ph5[("oev", NE - 1, 1)]]
    pop("zsb", "zsa", "mrow2", "oti1", "oti0", "ztp1", "ztp0", "zt1",
        "zt0", "qcol1", "qcol0")
    pop("V_sb", "K_sb")

    if upto < 6:
        b.emit()
        return nc

    OTh = push("OTh", [128, NE, 1024], BF16)
    stile = [push(f"sti{i}", [128, 1024], BF16) for i in range(2)]
    wcol6 = [push(f"wc6{i}", [128, NE, 128], BF16) for i in range(2)]
    TUA = push("TUA", [128, NE, E], BF16, is_transpose_dest=True, side="right")
    TUB = push("TUB", [128, NE, E], BF16, is_transpose_dest=True, side="right")

    def up_unit(ic):
        buf = wtern[ic % 2]
        w = [ph7[("dtern", 30 + ic)]] if ic < 2 else [ph7[("tern", ic - 2)]]
        ld = b.dma("sync", f"tu{ic % 2}", buf[:], wup_t[ic][:, :], waits=w)
        w = [ld, ph7[("dtern", 31)] if ic == 0 else ph7[("tern", ic - 1)]]
        msk = b.op("vector", lambda e, buf=buf: e.tensor_single_scalar(
            out=mskc[:], in_=buf[:], scalar=bcast[:, 1:2], op=ALU.is_le),
            waits=w)
        tt = ttile[ic % 2]
        w2 = [msk]
        w2.append(ph7[("dT", 30 + ic)] if ic < 2 else ph7[("tT", ic - 2)])
        ph7[("tern", ic)] = b.op("vector", lambda e, buf=buf, tt=tt:
            e.scalar_tensor_tensor(
                out=tt[:], in0=buf[:], scalar=bcast[:, 0:1],
                in1=mskc[:], op0=ALU.is_ge, op1=ALU.subtract), waits=w2)
        dst = TUA if ic < 16 else TUB
        j = ic % 16
        ph7[("tT", ic)] = b.dmat("sync", f"tT{ic % 2}",
                                 dst[:, :, j * 128:(j + 1) * 128],
                                 tt[:], waits=[ph7[("tern", ic)]])

    ph6 = {}
    uit = 0
    for hf in range(2):
        olds = []
        for ce in range(NE):
            w = [ph5[("ost", ce)]] + (list(ph5_done) if hf == 0 else [])
            if hf == 1:
                w.append(ph6[("mm", 0, NG - 1, 1)])
            olds.append(b.dma("sync", "ol", OTh[:, ce, :],
                              ot_d[ce][:, hf * 1024:(hf + 1) * 1024], waits=w))
        for gt in range(NG):
            buf6 = wcol6[gt % 2]
            w = []
            if gt >= 2:
                w.append(ph6[("mm", hf, gt - 2, 1)])
            elif hf == 1:
                w.append(ph6[("mm", 0, NG - 2 + gt, 1)])
            else:
                w += list(ph5_done)
            ldw = b.dma("gpsimd", f"w6{gt % 2}", buf6[:],
                        wo_d[:, gt * 128:(gt + 1) * 128]
                        .rearrange("(c p) f -> p c f", p=128), waits=w)
            for sb_ in range(2):
                pe = pA if gt % 2 == 0 else pB
                w2 = list(olds) + [ldw]
                if gt >= 2:
                    w2.append(ph6[("sev", hf, gt - 2)])
                elif hf == 0 and gt < 2:
                    w2 += [ph5[("oev", NE - 1, s_)] for s_ in range(4)]
                    w2 += [ph5[("exph", NE - 1, 0)], ph5[("exph", NE - 1, 1)],
                           ph5[("exph", NE - 2, 1)]]
                elif hf == 1:
                    w2.append(ph6[("sev", 0, NG - 2 + gt)])
                for ce in range(NE):
                    tok = b.op("tensor",
                               lambda e, pe=pe, ce=ce, sb_=sb_, buf6=buf6:
                               e.matmul(pe[:, sb_ * 512:(sb_ + 1) * 512],
                                        buf6[:, ce, :],
                                        OTh[:, ce, sb_ * 512:(sb_ + 1) * 512],
                                        start=(ce == 0), stop=(ce == NE - 1)),
                               waits=w2 if ce == 0 else ())
                ph6[("mm", hf, gt, sb_)] = tok
            stl = stile[gt % 2]
            w3 = [ph6[("mm", hf, gt, 1)], ph6[("mm", hf, gt, 0)]]
            if gt >= 2:
                w3.append(ph6[("sst", hf, gt - 2)])
            elif hf == 1:
                w3.append(ph6[("sst", 0, NG - 2 + gt)])
            ph6[("sev", hf, gt)] = b.op(
                "scalar", lambda e, pe=pe, stl=stl: e.activation(
                    out=stl[:], in_=pe[:, 0:1024], func=AF.Copy), waits=w3)
            ph6[("sst", hf, gt)] = b.dma(
                "scalar", f"ss{gt % 2}",
                sat_d[hf * 8:(hf + 1) * 8]
                .rearrange("t p c j -> p t c j")[:, :, gt, :],
                stl[:].rearrange("p (t j) -> p t j", j=128),
                waits=[ph6[("sev", hf, gt)]])
            if uit < NI:
                up_unit(uit)
                uit += 1
    satall = [ph6[("sst", 1, NG - 1)], ph6[("sst", 1, NG - 2)]]
    ph6_last_mm = ph6[("mm", 1, NG - 1, 1)]
    ph6_lastev = {0: ph6[("sev", 1, NG - 2)], 1: ph6[("sev", 1, NG - 1)]}
    tup_done = [ph7[("tT", NI - 1)], ph7[("tT", NI - 2)]]
    ph7_done = list(tup_done) + [ph7[("tern", NI - 1)],
                                 ph7[("tern", NI - 2)]] + dn_tail
    pop("wc61", "wc60", "sti1", "sti0", "OTh")
    pop("wst1b", "wst0b", "tstg", "tt1", "tt0", "mskc")

    if upto < 8:
        b.emit()
        return nc

    xqc = [push(f"xqc{i}", [128, NE, 128], BF16) for i in range(2)]
    hqb = push("hqb", [128, I], BF16)
    hqs = push("hqs", [128, NI, 128], BF16, is_transpose_dest=True)
    rbufs = [push(f"rbuf{i}", [128, I], F32) for i in range(2)]
    hibuf = push("hibuf", [128, I], mybir.dt.int32)
    dbuf = push("dbuf", [128, 1024], F32)
    ph8 = {}

    def emit_H(st):
        rbuf = rbufs[st % 2]
        xc = xqc[st % 2]
        w = [T[("xqS", st)]]
        if st < 2:
            w += ph7_done + [ph6_last_mm] + satall
        if st >= 2:
            w.append(ph8[("hmm", st - 2, 1)])
        ph8[("xqld", st)] = b.dma("gpsimd", f"xq{st % 2}", xc[:], xq_d[st],
                                  waits=w)
        for ih in range(2):
            pe = pA if ih == 0 else pB
            TU = TUA if ih == 0 else TUB
            w2 = [ph8[("xqld", st)]]
            if st >= 1:
                w2.append(ph8[("relu", st - 1, ih)])
            else:
                w2.append(bctok)
                w2.append(ph6_lastev[ih])
            first = True
            for ib in range(4):
                for ce in range(NE):
                    tok = b.op("tensor",
                               lambda e, pe=pe, ib=ib, ce=ce, xc=xc, TU=TU:
                               e.matmul(pe[:, ib * 512:(ib + 1) * 512],
                                        xc[:, ce, :],
                                        TU[:, ce, ib * 512:(ib + 1) * 512],
                                        start=(ce == 0), stop=(ce == NE - 1)),
                               waits=w2 if first else ())
                    first = False
            ph8[("hmm", st, ih)] = tok
            w3 = [tok]
            if st >= 2:
                w3.append(ph8[("lastt", st - 2)])
            ph8[("relu", st, ih)] = b.op(
                "scalar", lambda e, pe=pe, ih=ih, rbuf=rbuf: e.activation(
                    out=rbuf[:, ih * 2048:(ih + 1) * 2048], in_=pe[:, :],
                    func=AF.Relu), waits=w3)

    def emit_quant(st):
        rbuf = rbufs[st % 2]
        r0 = b.op("vector", lambda e, rbuf=rbuf: e.reduce_max(
            out=rmax[:, 0:1], in_=rbuf[:, 0:2048], axis=AX),
            waits=[ph8[("relu", st, 0)]] +
                  ([ph8[("rm", st - 1)]] if st >= 1 else []))
        r1 = b.op("vector", lambda e, rbuf=rbuf: e.reduce_max(
            out=rmax[:, 1:2], in_=rbuf[:, 2048:4096], axis=AX),
            waits=[ph8[("relu", st, 1)]])
        rm = b.op("vector", lambda e: e.tensor_max(
            out=rmx[:], in0=rmax[:, 0:1], in1=rmax[:, 1:2]), waits=[r0, r1])
        ph8[("rm", st)] = rm
        w = [rm]
        if st >= 1:
            w.append(ph8[("qs2", st - 1)])
        t1tok = b.op("vector", lambda e, st=st: e.tensor_mul(
            out=t1[:], in0=bcast[:, 4:5], in1=qr_all[:, st:st + 1]), waits=w)
        tok = b.op("vector", lambda e: e.tensor_mul(
            out=maxh[:], in0=t1[:], in1=rmx[:]),
            waits=[t1tok] + ([ph8[("cv", st - 1)]] if st >= 1 else []))
        tok = b.op("vector", lambda e: e.tensor_mul(
            out=maxh[:], in0=maxh[:], in1=maxh[:]), waits=[tok])
        tok = b.op("vector", lambda e: e.tensor_scalar_max(
            out=maxh[:], in0=maxh[:], scalar1=EPS), waits=[tok])
        tok_r = b.op("vector", lambda e: e.reciprocal(
            out=rcph[:], in_=maxh[:]), waits=[tok])
        tok = b.op("vector", lambda e: e.tensor_mul(
            out=u1[:], in0=t1[:], in1=t1[:]), waits=[tok_r])
        ph8[("qs2", st)] = b.op("vector", lambda e: e.tensor_scalar(
            out=qs2[:], in0=u1[:], scalar1=rcph[:], scalar2=127.0,
            op0=ALU.mult, op1=ALU.mult),
            waits=[tok] + ([ph8[("sqs", st - 1)]] if st >= 1 else []))
        ph8[("cv", st)] = b.op("vector", lambda e: e.tensor_scalar(
            out=cvec[:], in0=maxh[:], scalar1=bcast[:, 5:6],
            scalar2=1.0 / 127.0, op0=ALU.mult, op1=ALU.mult),
            waits=[tok_r] + ([ph8[("hq", st - 1, 3)]] if st >= 1 else []))
        ph8[("sqs", st)] = b.op("scalar", lambda e: e.activation(
            out=sqsb[:], in_=qs2[:], func=AF.Sqrt),
            waits=[ph8[("qs2", st)]] +
                  ([ph8[("sqf", st - 1)]] if st >= 1 else []))
        ph8[("sqf", st)] = b.op("scalar", lambda e, rbuf=rbuf: e.activation(
            out=rbuf[:, :], in_=rbuf[:, :], func=AF.Square, scale=sqsb[:]),
            waits=[ph8[("sqs", st)], r0, r1])
        y2 = b.op("vector", lambda e, rbuf=rbuf: e.tensor_scalar_add(
            out=rbuf[:, :], in0=rbuf[:, :], scalar1=0.5),
            waits=[ph8[("sqf", st)]])
        yi = b.op("vector", lambda e, rbuf=rbuf: e.tensor_copy(
            out=hibuf[:], in_=rbuf[:, :]),
            waits=[y2] + ([ph8[("hq", st - 1, 3)]] if st >= 1 else []))
        d4 = b.op("vector", lambda e, rbuf=rbuf: e.tensor_sub(
            out=rbuf[:, :], in0=rbuf[:, :], in1=hibuf[:]), waits=[yi])
        prev = d4
        for ch in range(4):
            sl = slice(ch * 1024, (ch + 1) * 1024)
            c1 = b.op("vector", lambda e, sl=sl, rbuf=rbuf:
                      e.tensor_single_scalar(
                          out=dbuf[:], in_=rbuf[:, sl], scalar=1.0,
                          op=ALU.is_ge), waits=[prev])
            c2 = b.op("vector", lambda e, sl=sl, rbuf=rbuf:
                      e.scalar_tensor_tensor(
                          out=dbuf[:], in0=rbuf[:, sl], scalar=0.0,
                          in1=dbuf[:], op0=ALU.is_lt, op1=ALU.subtract),
                      waits=[c1])
            mag = b.op("vector", lambda e, sl=sl, rbuf=rbuf: e.tensor_sub(
                out=rbuf[:, sl], in0=hibuf[:, sl], in1=dbuf[:]), waits=[c2])
            wq_ = [mag, ph8[("cv", st)]]
            if ch == 0 and st >= 1:
                wq_.append(ph8[("hqT", st - 1)])
            ph8[("hq", st, ch)] = b.op("vector",
                lambda e, sl=sl, rbuf=rbuf: e.tensor_single_scalar(
                    out=hqb[:, sl], in_=rbuf[:, sl], scalar=cvec[:],
                    op=ALU.mult), waits=wq_)
            prev = ph8[("hq", st, ch)]
        ph8[("lastt", st)] = ph8[("hq", st, 3)]
        w5 = [ph8[("hq", st, c_)] for c_ in range(4)]
        if st >= 1:
            w5.append(ph8[("hqSt", st - 1)])
        tok = b.dmat("sync", "hqT", hqs[:], hqb[:], waits=w5)
        ph8[("hqT", st)] = tok
        ph8[("hqSt", st)] = b.dma(
            "scalar", "hqS", hq_d[:, :, st * 128:(st + 1) * 128], hqs[:],
            waits=[tok])

    for st in range(NS):
        emit_H(st)
        if st >= 1:
            emit_quant(st - 1)
    emit_quant(NS - 1)
    f1_done = [ph8[("hmm", NS - 1, 1)], ph8[("hmm", NS - 1, 0)],
               ph8[("hqSt", NS - 1)], ph8[("hqT", NS - 1)],
               ph8[("lastt", NS - 1)], ph8[("relu", NS - 1, 0)],
               ph8[("relu", NS - 1, 1)], ph8[("qs2", NS - 1)],
               ph8[("cv", NS - 1)], ph8[("rm", NS - 1)],
               ph8[("sqs", NS - 1)], ph8[("sqf", NS - 1)]]
    pop("dbuf", "hibuf", "rbuf1", "rbuf0",
        "hqs", "hqb", "xqc1", "xqc0")
    pop("TUB", "TUA")

    HQf = push("HQf", [128, NI, S], BF16, side="right")
    tdts = [push(f"tdts{i}", [128, NI, 128], BF16) for i in range(2)]
    gtile = [push(f"gt{i}", [128, S], BF16) for i in range(2)]
    hqls = []
    for st in range(NS):
        w = [ph8[("hqSt", st)], ph8[("hmm", NS - 1, 0)],
             ph8[("hmm", NS - 1, 1)]]
        hqls.append(b.dma("gpsimd", "hql",
                          HQf[:, :, st * 128:(st + 1) * 128],
                          hq_d[:, :, st * 128:(st + 1) * 128], waits=w))
    hql = hqls[-1]
    for gt in range(NG):
        tb_ = tdts[gt % 2]
        w = [hql]
        if gt < 2:
            w.append(tdn_done)
            w += f1_done
        if gt >= 2:
            w.append(ph8[("fmm", gt - 2)])
        ld = b.dma("sync", f"td{gt % 2}", tb_[:], tdt_d[gt], waits=w)
        pe = pA if gt % 2 == 0 else pB
        w2 = [ld]
        if gt >= 2:
            w2.append(ph8[("gev", gt - 2)])
        else:
            w2.append(ph8[("relu", NS - 1, gt)])
        first = True
        for sb_ in range(4):
            for ci in range(NI):
                tok = b.op("tensor", lambda e, pe=pe, ci=ci, sb_=sb_, tb_=tb_:
                           e.matmul(pe[:, sb_ * 512:(sb_ + 1) * 512],
                                    tb_[:, ci, :],
                                    HQf[:, ci, sb_ * 512:(sb_ + 1) * 512],
                                    start=(ci == 0), stop=(ci == NI - 1)),
                           waits=w2 if first else ())
                first = False
        ph8[("fmm", gt)] = tok
        gq = gtile[gt % 2]
        w3 = [tok]
        if gt >= 2:
            w3.append(ph8[("gst", gt - 2)])
        ph8[("gev", gt)] = b.op("scalar", lambda e, pe=pe, gq=gq: e.activation(
            out=gq[:], in_=pe[:, :], func=AF.Silu), waits=w3)
        ph8[("gst", gt)] = b.dma(
            "scalar", f"gs{gt % 2}",
            g_d.rearrange("t p c j -> p t c j")[:, :, gt, :],
            gq[:].rearrange("p (t j) -> p t j", j=128),
            waits=[ph8[("gev", gt)]])
    f2_done = [ph8[("gst", NG - 1)], ph8[("gst", NG - 2)],
               ph8[("fmm", NG - 1)], ph8[("gev", NG - 1)],
               ph8[("gev", NG - 2)]]
    pop("gt1", "gt0", "tdts1", "tdts0", "HQf")

    if upto < 9:
        b.emit()
        return nc

    fss = [push(f"fss{i}", [128, NE, 128], BF16, is_transpose_dest=True)
           for i in range(2)]
    fsg = [push(f"fsg{i}", [128, NE, 128], BF16, is_transpose_dest=True)
           for i in range(2)]
    satm = [push(f"satm{i}", [128, E], BF16) for i in range(2)]
    gtm = [push(f"gtm{i}", [128, E], BF16) for i in range(2)]
    sgb = [push(f"sgb{i}", [128, E], BF16) for i in range(2)]
    xinf = [push(f"xinf{i}", [128, E], F32) for i in range(2)]
    outt = [push(f"outt{i}", [128, E], F32) for i in range(2)]
    ph9 = {}
    for st in range(NS):
        w = list(satall) + (list(f2_done) if st < 2 else [])
        if st >= 2:
            w.append(ph9[("sT", st - 2)])
        sld = b.dma("sync", f"f9s{st % 2}", fss[st % 2][:], sat_d[st],
                    waits=w)
        w = [sld]
        if st >= 2:
            w.append(ph9[("sg", st - 2)])
        ph9[("sT", st)] = b.dmat(
            "sync", f"f9s{st % 2}",
            satm[st % 2][:].rearrange("p (c j) -> p c j", j=128),
            fss[st % 2][:].rearrange("p c j -> p (c j)"), waits=w)
        w = (list(f2_done) if st < 2 else [ph9[("gT", st - 2)]])
        gld = b.dma("scalar", f"f9g{st % 2}", fsg[st % 2][:], g_d[st],
                    waits=w)
        w = [gld]
        if st >= 2:
            w.append(ph9[("sg", st - 2)])
        ph9[("gT", st)] = b.dmat(
            "sync", f"f9t{st % 2}",
            gtm[st % 2][:].rearrange("p (c j) -> p c j", j=128),
            fsg[st % 2][:].rearrange("p c j -> p (c j)"), waits=w)
        w2 = (list(f2_done) if st < 2 else [ph9[("add", st - 2)]])
        xl = b.dma("gpsimd", f"x9{st % 2}", xinf[st % 2][:], x_t[st],
                   waits=w2)
        w3 = [ph9[("sT", st)], ph9[("gT", st)]]
        if st >= 2:
            w3.append(ph9[("add", st - 2)])
        ph9[("sg", st)] = b.op("vector",
            lambda e, st=st: e.tensor_mul(
                out=sgb[st % 2][:], in0=satm[st % 2][:],
                in1=gtm[st % 2][:]), waits=w3)
        ob = outt[st % 2]
        w4 = [ph9[("sg", st)], xl]
        if st >= 2:
            w4.append(ph9[("ost", st - 2)])
        ph9[("add", st)] = b.op("vector", lambda e, ob=ob, st=st: e.tensor_add(
            out=ob[:], in0=xinf[st % 2][:], in1=sgb[st % 2][:]), waits=w4)
        ph9[("ost", st)] = b.dma(
            "sync", f"oo{st % 2}",
            out_d.rearrange("(t p) e -> t p e", p=128)[st], ob[:],
            waits=[ph9[("add", st)]])

    b.waits("sync", [ph9[("ost", NS - 1)], ph9[("ost", NS - 2)]])
    b.emit()
    return nc


def kernel(L0_attn_x, L0_attn_Wq, L0_attn_Wk, L0_attn_Wv, L0_attn_Wo,
           L0_ffn_w_up, L0_ffn_w_down):
    xs = np.ascontiguousarray(np.asarray(L0_attn_x, dtype=np.float32))
    ws = {
        "wq": np.ascontiguousarray(np.asarray(L0_attn_Wq, dtype=np.float32)),
        "wk": np.ascontiguousarray(np.asarray(L0_attn_Wk, dtype=np.float32)),
        "wv": np.ascontiguousarray(np.asarray(L0_attn_Wv, dtype=np.float32)),
        "wo": np.ascontiguousarray(np.asarray(L0_attn_Wo, dtype=np.float32)),
        "wup": np.ascontiguousarray(np.asarray(L0_ffn_w_up, dtype=np.float32)),
        "wdn": np.ascontiguousarray(np.asarray(L0_ffn_w_down, dtype=np.float32)),
    }
    nc = build_program()
    in_maps = [dict(ws, x=xs[i]) for i in range(8)]
    res = run_bass_kernel_spmd(nc, in_maps, core_ids=list(range(8)))
    return np.stack([np.asarray(r["out"]) for r in res.results], axis=0)
